# version 16
# baseline (speedup 1.0000x reference)
import contextlib
import numpy as np
import ml_dtypes
import concourse.bass as bass
import concourse.mybir as mybir
from concourse.bass_utils import run_bass_kernel_spmd

F32 = mybir.dt.float32
BF16 = mybir.dt.bfloat16
ALU = mybir.AluOpType
ACT = mybir.ActivationFunctionType
AX = mybir.AxisListType
BFNP = ml_dtypes.bfloat16

B, T, D = 4, 4096, 1024
H, K, C = 16, 64, 64
NT = 2048
NCH = 32
E = 8
NSG = 4
NCORE = 8

ENGS = ("tensor", "vector", "scalar", "gpsimd", "sync")
OPT = {"upto": "all", "taps": ()}


class Sched:
    def __init__(self):
        self.ops = {e: [] for e in ENGS}
        self.cnt = {e: 0 for e in ENGS}
        self.dcnt = {"sync": 0, "gpsimd": 0, "scalar": 0}
        self.last_wait = {e: {} for e in ENGS}
        self.pending = {e: [] for e in ENGS}
        self.dirty = False

    def emit(self, eng, fn, waits=(), tick=False):
        w = list(self.pending[eng]) + [x for x in waits if x is not None]
        self.pending[eng] = []
        dedup = []
        for sk, val in w:
            if val > self.last_wait[eng].get(sk, -1):
                dedup.append((sk, val))
                self.last_wait[eng][sk] = val
        entry = [dedup, fn, [], False]
        self.ops[eng].append(entry)
        self.dirty = True
        if tick:
            self.cnt[eng] += 1
            entry[2].append((eng + "_c", 1))
            return (eng + "_c", self.cnt[eng])
        return None

    def dma(self, eng, fn, waits=()):
        self.emit(eng, fn, waits=waits)
        self.dcnt[eng] += 16
        self.ops[eng][-1][2].append((eng + "_d", 16))
        self.ops[eng][-1][3] = True
        return (eng + "_d", self.dcnt[eng])

    def tick_last(self, eng):
        if not self.ops[eng]:
            return None
        entry = self.ops[eng][-1]
        if entry[3]:
            return None
        for sk, _ in entry[2]:
            if sk == eng + "_c":
                return (eng + "_c", self.cnt[eng])
        self.cnt[eng] += 1
        entry[2].append((eng + "_c", 1))
        return (eng + "_c", self.cnt[eng])

    def fence(self, engs=None, to=None):
        if not self.dirty and engs is None and to is None:
            return []
        if engs is None and to is None:
            self.dirty = False
        evs = []
        for e in (engs or ENGS):
            if e in self.dcnt and self.dcnt[e]:
                evs.append((e + "_d", self.dcnt[e]))
            ev = self.tick_last(e)
            if ev is not None:
                evs.append(ev)
        for e in (to or ENGS):
            self.pending[e].extend(evs)
        return evs


class Pool:
    def __init__(self, tensors, name):
        self.t = list(tensors)
        self.free = list(range(len(self.t)))
        self.used = {}
        self.name = name
        self.hwm = 0
        self.sched = None

    def get(self, tag):
        assert self.free, (f"pool {self.name} empty for {tag}; "
                           f"used={sorted(self.used.values())}")
        i = self.free.pop()
        self.used[i] = tag
        self.hwm = max(self.hwm, len(self.used))
        return self.t[i]

    def put(self, t):
        if self.sched is not None:
            self.sched.fence()
        for i, tt in enumerate(self.t):
            if tt is t:
                del self.used[i]
                self.free.append(i)
                return
        raise KeyError("not a pool tensor")


def bc(ap2d, n, reps):
    p = ap2d.shape[0]
    return ap2d.rearrange("p (n o) -> p n o", o=1).broadcast_to((p, n, reps))


class Builder:
    def __init__(self):
        self.nc = bass.Bass("TRN2", target_bir_lowering=False, debug=False)
        self.S = Sched()
        self.ctx = contextlib.ExitStack()
        self.dbg_col = 0
        self.taps = list(OPT["taps"])
        self.bank_free = {}
        self.bt_free = {}

    def sb(self, n, s, dt):
        return self.ctx.enter_context(self.nc.sbuf_tensor(n, list(s), dt))

    def ps(self, n, s, dt):
        return self.ctx.enter_context(self.nc.psum_tensor(n, list(s), dt))

    def tap(self, name, ap):
        if name not in self.taps or self.dbg_col >= 512:
            return
        w = min(int(np.prod(ap.shape[1:])), 512 - self.dbg_col)
        c0 = self.dbg_col
        self.dbg_col += w
        rows = ap.shape[0]
        if len(ap.shape) != 2 or ap.shape[1] != w:
            if len(ap.shape) == 2:
                ap = ap[:, 0:w]
            else:
                return
        self.S.fence()
        self.S.emit("vector", lambda v, ap=ap, c0=c0, w=w, rows=rows:
                    v.tensor_copy(self.dbgsb[0:rows, c0:c0 + w], ap), tick=True)
        self.S.fence()

    def declare(self):
        nc = self.nc
        din = lambda name, shape, dt=F32: nc.dram_tensor(
            name, list(shape), dt, kind="ExternalInput").ap()
        self.x_h = din("x", (NT, D))
        self.w_hs = {q: din("w" + q, (D, D)) for q in "qkvgo"}
        self.wb_h = din("wb", (D, 16)); self.wa_h = din("wa", (D, 16))
        self.dtb_h = din("dtb", (16, 1)); self.alog_h = din("alog", (16, 1))
        self.normw_h = din("normw", (128, 1))
        self.we_h = din("we", (E, D, D)); self.be_h = din("be", (E, D))
        self.wgf_h = din("wgf", (D, E)); self.bgf_h = din("bgf", (8, 1))
        self.idbf_h = din("idbf", (128, 128), BF16)
        self.idf_h = din("idf", (128, 128), F32)
        self.hsel_h = din("hsel", (128, 2), BF16)
        self.mstrict_h = din("mstrict", (128, 64), BF16)
        self.mustrict_h = din("mustrict", (128, 64), BF16)
        self.mutril_h = din("mutril", (128, 64), BF16)
        self.imaskr_h = din("imaskr", (128, 64), BF16)
        self.recvm_h = din("recvm", (128, 8), F32)
        self.out_h = nc.dram_tensor("out", [NT, D], F32, kind="ExternalOutput").ap()
        self.dbg_h = nc.dram_tensor("dbg", [128, 512], F32, kind="ExternalOutput").ap()
        self.rows_h = nc.dram_tensor("rows_i", [128, NT], F32, kind="Internal").ap()
        self.rrows_h = nc.dram_tensor("rrows_i", [32, NT], BF16, kind="Internal").ap()
        self.stash_h = nc.dram_tensor("stash_i", [6144, NT], BF16, kind="Internal").ap()
        self.rowsT_h = nc.dram_tensor("rowsT_i", [6 * 2048, 16], F32, kind="Internal").ap()
        self.ccin_h = nc.dram_tensor("ccin_i", [128, 512], BF16, kind="Internal").ap()
        self.ccout_h = nc.dram_tensor("ccout_i", [128 * NCORE, 512], BF16,
                                      kind="Internal").ap()

        self.xnT = self.sb("xnT", (128, 8 * NT), BF16)
        self.PB = Pool([self.sb(f"pb{i}", (128, NT), BF16) for i in range(21)], "PB")
        self.PF = Pool([self.sb(f"pf{i}", (128, NT), F32) for i in range(4)], "PF")
        self.PB.sched = self.S
        self.PF.sched = self.S

        s = self.sb
        self.idbf = s("idbf_s", (128, 128), BF16)
        self.idf = s("idf_s", (128, 128), F32)
        self.hsel = s("hsel_s", (128, 2), BF16)
        self.mstrict = s("mstrict_s", (128, 64), BF16)
        self.mustrict = s("mustrict_s", (128, 64), BF16)
        self.mutril = s("mutril_s", (128, 64), BF16)
        self.imaskr = s("imaskr_s", (128, 64), BF16)
        self.recvm = s("recvm_s", (128, 8), F32)
        self.normw = s("normw_s", (128, 1), F32)
        self.dtb = s("dtb_s", (16, 1), F32)
        self.alog = s("alog_s", (16, 1), F32)
        self.negea = s("negea", (16, 1), F32)
        self.beta_r = s("beta_r", (16, NT), F32)
        self.g_r = s("g_r", (16, NT), F32)
        self.sc1_r = s("sc1_r", (16, NT), F32)
        self.sp_r = self.sc1_r
        self.com_r = self.sc1_r
        self.glr = s("glr", (16, NT), F32)
        self.com2_r = s("com2_r", (16, NT), F32)
        self.zer16 = s("zer16", (16, C), F32)
        self.l2s = self.glr
        self.fpair = s("fpair", (128, 6 * 8 * 32), F32)
        self.S32 = s("S32", (128, 128), F32)
        self.Sbf = s("Sbf", (128, 512), BF16)

        self.wtok = s("wtok", (128, 128), F32)
        self.wsc = s("wsc", (128, 128), F32)
        self.logit = self.beta_r
        self.besb = s("besb_s", (8, D), BF16)
        self.bgfsb = s("bgfsb_s", (8, 1), F32)
        self.wgfsb = s("wgfsb_s", (128, 64), BF16)
        self.wbsb = s("wbsb_s", (128, 128), BF16)
        self.wasb = s("wasb_s", (128, 128), BF16)
        self.rmsq = s("rmsq", (128, 64), F32)
        self.etb = s("etb", (128, 512), BF16)
        self.qtb = s("qtb", (128, 512), BF16)
        self.dbgsb = s("dbgsb", (128, 512), F32)

        self.cvals = s("cvals", (128, 8), F32)
        self.pbank = [self.ps(f"pk{i}", (128, 512), F32) for i in range(6)]
        self.pbt = [self.ps(f"pt{i}", (128, 1024), BF16) for i in range(2)]

        self.sems = {}
        for e in ENGS:
            self.sems[e + "_c"] = self.ctx.enter_context(self.nc.semaphore(e + "_c"))
        for e in ("sync", "gpsimd", "scalar"):
            self.sems[e + "_d"] = self.ctx.enter_context(self.nc.semaphore(e + "_d"))

    def fp(self, q, pair, j0=0, nj=NCH):
        c0 = (q * 32 + j0) * 8 + pair
        return self.fpair[:, c0:c0 + nj * 8:8]

    def load_wslice(self, w_hbm, sg, tag, waits=()):
        unit = self.PB.get(tag)
        ev = self.S.dma("gpsimd", lambda g, un=unit, wh=w_hbm, sg=sg:
                        g.dma_start(
                            out=un.rearrange("p (d c) -> p d c", d=8),
                            in_=wh[:, sg * 256:(sg + 1) * 256]
                            .rearrange("(d p) c -> p d c", p=128)),
                        waits=waits)
        return unit, ev

    def load_w1024(self, w_hbm, tag, waits=()):
        units = [self.PB.get(f"{tag}{i}") for i in range(4)]
        ev = None
        for u in range(4):
            ev = self.S.dma("gpsimd", lambda g, un=units[u], wh=w_hbm, u=u:
                            g.dma_start(
                                out=un.rearrange("p (a c) -> p a c", a=2),
                                in_=wh[u * 256:(u + 1) * 256, :]
                                .rearrange("(a p) c -> p a c", p=128)),
                            waits=waits)
        return units, ev

    @staticmethod
    def wsl(units, d, blk, width=128):
        return units[d // 2][:, (d % 2) * 1024 + blk * width:
                             (d % 2) * 1024 + (blk + 1) * width]

    def bank(self, i):
        return self.pbank[i], self.bank_free.get(i, [])

    def build(self):
        self.declare()
        self.p0_consts()
        self.p1_ln()
        self.p2_rows()
        if OPT["upto"] == "ln":
            return self.finish()
        self.ofin = []
        self.sgd_all = {}
        for sg in range(OPT.get("nsg", NSG)):
            self.attn_sg(sg)
        self.scan_phase()
        if OPT["upto"] == "attn":
            return self.finish()
        for half in (0, 1):
            if half == 0:
                self.wo_prefetch()
            self.wo_phase(half)
            if half == 0:
                self.tap("accA", self.ACC[0][:, 0:128])
            self.ffn_phase(half)
            if half == 0:
                self.tap("accB", self.ACC[0][:, 0:128])
            self.out_half(half)
        return self.finish()

    def p0_consts(self):
        S = self.S
        cev = None
        for dst, src in ((self.idbf, self.idbf_h), (self.idf, self.idf_h),
                         (self.hsel, self.hsel_h), (self.mstrict, self.mstrict_h),
                         (self.mustrict, self.mustrict_h), (self.mutril, self.mutril_h),
                         (self.imaskr, self.imaskr_h), (self.recvm, self.recvm_h),
                         (self.normw, self.normw_h), (self.dtb, self.dtb_h),
                         (self.alog, self.alog_h), (self.bgfsb, self.bgf_h)):
            cev = S.dma("sync", lambda sy, d=dst, s=src:
                        sy.dma_start(out=d[:, :], in_=s[:, :]))
        self.consts_s = cev
        gev = S.dma("gpsimd", lambda g: g.dma_start(out=self.besb[:, :],
                                                    in_=self.be_h[:, :]))
        for dst, src in ((self.wgfsb, self.wgf_h), (self.wbsb, self.wb_h),
                         (self.wasb, self.wa_h)):
            gev = S.dma("gpsimd", lambda g, d=dst, s=src: g.dma_start(
                out=d.rearrange("p (a e) -> p a e", a=8),
                in_=s.rearrange("(a p) e -> p a e", p=128)))
        self.consts_g = gev
        self.CONSTS = {1e-5: 0, 64e-6: 1, 1e-6: 2, -1.0: 3, 1.0 / 64: 4,
                       64.0: 5, 1.0: 6, 0.0: 7}
        for val, i in self.CONSTS.items():
            S.emit("gpsimd", lambda g, val=val, i=i:
                   g.memset(self.cvals[:, i:i + 1], float(val)), tick=True)
            self.nc.const_aps.register(F32, float(val), self.cvals[:, i:i + 1])
        S.fence(["gpsimd"])
        S.emit("gpsimd", lambda g: g.memset(self.zer16[:, :], 0.0), tick=True)
        S.emit("gpsimd", lambda g: g.memset(self.dbgsb[:, :], 0.0), tick=True)
        self.xcopy_ev = S.dma("sync", lambda sy:
                              sy.dma_start(out=self.out_h[:, :], in_=self.x_h[:, :]))
        ev = S.emit("scalar", lambda s: s.activation(self.negea[:, :],
                                                     self.alog[:, :], ACT.Exp),
                    waits=[self.consts_s], tick=True)
        S.emit("vector", lambda v: v.tensor_scalar_mul(self.negea[:, :],
                                                       self.negea[:, :], -1.0),
               waits=[ev], tick=True)

    def p1_ln(self):
        S, PF, PB = self.S, self.PF, self.PB
        xt = [PF.get(f"x{i}") for i in range(2)]
        sqt = PF.get("sq")
        stat = PF.get("stat")
        xn_tmp = [PB.get(f"xn{i}") for i in range(2)]
        e9d, tpmm, tpev, sqred = {}, {}, {}, {}
        for t in range(16):
            buf = xt[t % 2]
            ld = S.dma("sync", lambda sy, b=buf, t=t:
                       sy.dma_start(out=b[:, 0:1024],
                                    in_=self.x_h[t * 128:(t + 1) * 128, :]),
                       waits=[e9d.get(t - 2)])
            c = t * 8
            e1 = S.emit("vector", lambda v, b=buf, c=c:
                        v.reduce_sum(stat[:, c:c + 1], b[:, 0:1024], AX.X),
                        waits=[ld], tick=True)
            e2 = S.emit("scalar", lambda s, b=buf:
                        s.activation(sqt[:, 0:1024], b[:, 0:1024], ACT.Square),
                        waits=[ld, sqred.get(t - 1)], tick=True)
            e3 = S.emit("vector", lambda v, c=c:
                        v.reduce_sum(stat[:, c + 1:c + 2], sqt[:, 0:1024], AX.X),
                        waits=[e2], tick=True)
            sqred[t] = e3
            e4 = S.emit("vector", lambda v, c=c:
                        v.tensor_scalar_mul(stat[:, c + 2:c + 3], stat[:, c:c + 1],
                                            -1.0 / 1024), waits=[e1], tick=True)
            e5 = S.emit("vector", lambda v, c=c:
                        v.tensor_tensor(stat[:, c + 3:c + 4], stat[:, c + 2:c + 3],
                                        stat[:, c + 2:c + 3], ALU.mult),
                        waits=[e4], tick=True)
            e6 = S.emit("vector", lambda v, c=c:
                        v.scalar_tensor_tensor(stat[:, c + 4:c + 5],
                                               stat[:, c + 1:c + 2], 1.0 / 1024,
                                               stat[:, c + 3:c + 4],
                                               ALU.mult, ALU.subtract),
                        waits=[e3, e5], tick=True)
            e7 = S.emit("scalar", lambda s, c=c:
                        s.activation(stat[:, c + 5:c + 6], stat[:, c + 4:c + 5],
                                     ACT.Sqrt, bias=1e-5), waits=[e6], tick=True)
            e8 = S.emit("vector", lambda v, c=c:
                        v.reciprocal(stat[:, c + 6:c + 7], stat[:, c + 5:c + 6]),
                        waits=[e7], tick=True)
            xnb = xn_tmp[t % 2]
            e9 = S.emit("vector", lambda v, b=buf, xb=xnb, c=c:
                        v.tensor_scalar(xb[:, 0:1024], b[:, 0:1024],
                                        stat[:, c + 2:c + 3], stat[:, c + 6:c + 7],
                                        ALU.add, ALU.mult),
                        waits=[e8, tpmm.get(t - 2)], tick=True)
            e9d[t] = e9
            bank = self.pbt[t % 2]
            last = None
            for dd in range(8):
                last = S.emit("tensor", lambda tn, xb=xnb, dd=dd, bank=bank:
                              tn.transpose(bank[:, dd * 128:(dd + 1) * 128],
                                           xb[:, dd * 128:(dd + 1) * 128],
                                           self.idbf[:, :]),
                              waits=[e9, self.consts_s, tpev.get(t - 2)],
                              tick=(dd == 7))
            tpmm[t] = last
            tpev[t] = S.emit("scalar", lambda s, bank=bank, t=t:
                             s.copy(self.xnT.rearrange("p (d n) -> p d n", d=8)
                                    [:, :, t * 128:(t + 1) * 128],
                                    bank.rearrange("p (d n) -> p d n", d=8)),
                             waits=[last], tick=True)
        S.fence()
        self.tap("xt15", xt[15 % 2][:, 0:64])
        self.tap("stat", stat[:, 0:128])
        self.tap("xn15", xn_tmp[15 % 2][:, 0:64])
        for u in xt + [sqt, stat]:
            PF.put(u)
        for u in xn_tmp:
            PB.put(u)
        self.tap("xnT", self.xnT[:, 0:128])

    def proj_sg0(self):
        S, PB = self.S, self.PB
        pairs = (0, 1)
        qT, kT, gT, vT = {}, {}, {}, {}
        wus, wevs = {}, {}
        for qty in "qkgv":
            wus[qty], wevs[qty] = self.load_wslice(self.w_hs[qty], 0, f"w{qty}")
        for qty, store in (("q", qT), ("k", kT), ("g", gT), ("v", vT)):
            for p in pairs:
                store[p] = PB.get(f"{qty}T{p}")
            self.proj_pairs(wus[qty], wevs[qty], pairs, store)
        S.fence(["tensor", "vector", "scalar"])
        for qty in "qkgv":
            PB.put(wus[qty])
        for p in pairs:
            S.dma("sync", lambda sy, p=p: sy.dma_start(
                out=self.stash_h[p * 128:(p + 1) * 128, :], in_=gT[p][:, :]))
        S.fence(["sync"])
        for p in pairs:
            PB.put(gT[p])
        self._sg0 = (qT, kT, vT)

    def p2_rows(self):
        NT_ = ("sync", "vector", "scalar", "gpsimd")
        S = self.S
        for wi, (wsb, mode) in enumerate(((self.wbsb, "sig"), (self.wasb, "sp"))):
            for tc in range(4):
                bk, bev = self.bank((wi * 4 + tc) % 2)
                ev = None
                for d in range(8):
                    ev = S.emit("tensor", lambda tn, wsb=wsb, d=d, tc=tc, bk=bk:
                                tn.matmul(bk[0:16, 0:512],
                                          wsb[:, d * 16:(d + 1) * 16],
                                          self.xnT[:, d * NT + tc * 512:
                                                   d * NT + (tc + 1) * 512],
                                          start=(d == 0), stop=(d == 7)),
                                waits=([self.consts_g] + bev) if d == 0 else [],
                                tick=(d == 7))
                if mode == "sig":
                    e2 = S.emit("scalar", lambda s, bk=bk, tc=tc:
                                s.activation(self.beta_r[:, tc * 512:(tc + 1) * 512],
                                             bk[0:16, 0:512], ACT.Sigmoid),
                                waits=[ev], tick=True)
                else:
                    e2a = S.emit("scalar", lambda s, bk=bk, tc=tc:
                                 s.activation(self.sp_r[:, tc * 512:(tc + 1) * 512],
                                              bk[0:16, 0:512], ACT.Exp,
                                              bias=self.dtb[:, :]),
                                 waits=[ev], tick=True)
                    e2 = S.emit("scalar", lambda s, tc=tc:
                                s.activation(self.sp_r[:, tc * 512:(tc + 1) * 512],
                                             self.sp_r[:, tc * 512:(tc + 1) * 512],
                                             ACT.Ln, bias=1.0),
                                waits=[e2a], tick=True)
                self.bank_free[(wi * 4 + tc) % 2] = [e2]
        S.fence()
        S.emit("vector", lambda v: v.tensor_scalar(self.sc1_r[:, :], self.sc1_r[:, :],
                                                   self.negea[:, :], None, ALU.mult),
               tick=True)
        S.fence()
        for j in range(NCH):
            S.emit("vector", lambda v, j=j:
                   v.tensor_tensor_scan(self.g_r[:, j * C:(j + 1) * C],
                                        self.sp_r[:, j * C:(j + 1) * C],
                                        self.zer16[:, :], 0.0, ALU.add, ALU.add),
                   tick=(j == NCH - 1))
        S.fence()
        S.emit("vector", lambda v:
               v.tensor_copy(self.glr.rearrange("p (j o) -> p j o", o=C),
                             self.g_r.rearrange("p (j o) -> p j o", o=C)
                             [:, :, C - 1:C].broadcast_to((16, NCH, C))),
               tick=True)
        S.fence()
        S.emit("scalar", lambda s: s.activation(self.sc1_r[:, :], self.g_r[:, :],
                                                ACT.Exp), tick=True)
        S.fence(["scalar"])
        S.dma("sync", lambda sy: sy.dma_start(out=self.rows_h[5:128:8, :],
                                              in_=self.sc1_r[:, :]))
        S.emit("vector", lambda v: v.tensor_tensor(self.com2_r[:, :], self.beta_r[:, :],
                                                   self.sc1_r[:, :], ALU.mult),
               tick=True)
        S.fence(["vector"])
        S.dma("sync", lambda sy: sy.dma_start(out=self.rows_h[1:128:8, :],
                                              in_=self.com2_r[:, :]))
        S.emit("vector", lambda v: v.tensor_tensor(self.com2_r[:, :], self.glr[:, :],
                                                   self.g_r[:, :], ALU.subtract),
               waits=[("sync_d", S.dcnt["sync"])], tick=True)
        S.fence(["vector"])
        S.emit("scalar", lambda s: s.activation(self.com2_r[:, :], self.com2_r[:, :],
                                                ACT.Exp), tick=True)
        S.fence(["scalar"])
        S.dma("sync", lambda sy: sy.dma_start(out=self.rows_h[2:128:8, :],
                                              in_=self.com2_r[:, :]))
        S.emit("scalar", lambda s: s.activation(self.sc1_r[:, :], self.glr[:, :],
                                                ACT.Exp),
               waits=[("sync_d", S.dcnt["sync"])], tick=True)
        S.fence(["scalar"])
        S.dma("sync", lambda sy: sy.dma_start(out=self.rows_h[3:128:8, :],
                                              in_=self.sc1_r[:, :]))
        S.dma("sync", lambda sy: sy.dma_start(out=self.rows_h[0:128:8, :],
                                              in_=self.beta_r[:, :]))
        S.dma("sync", lambda sy: sy.dma_start(out=self.rows_h[4:128:8, :],
                                              in_=self.g_r[:, :]))
        rows_ev = S.fence()
        rowsrc = {0: self.beta_r, 4: self.g_r, 5: None, 1: None, 2: None, 3: None}
        stg = self.PF.get("rowstg")
        for q in range(6):
            rt = self.com2_r
            wprev = [("sync_d", S.dcnt["sync"])]
            lev = S.dma("sync", lambda sy, q=q, rt=rt: sy.dma_start(
                out=rt[:, :], in_=self.rows_h[q:128:8, :]), waits=wprev)
            bk = self.pbank[0]
            ev = None
            for m in range(16):
                ev = S.emit("tensor", lambda tn, m=m, bk=bk, rt=rt:
                            tn.transpose(bk[:, m * 16:(m + 1) * 16],
                                         rt[0:16, m * 128:(m + 1) * 128],
                                         self.idf[0:16, 0:16]),
                            waits=([lev] + self.bank_free.get(0, []))
                            if m == 0 else [],
                            tick=(m == 15))
            e2 = S.emit("vector", lambda v, bk=bk:
                        v.tensor_copy(stg[:, 0:256]
                                      .rearrange("p (m t r) -> p m t r", m=16, t=2),
                                      bk[:, 0:256].rearrange("p (m r t) -> p m t r",
                                                             m=16, t=2)),
                        waits=[ev], tick=True)
            self.bank_free[0] = [e2]
            S.dma("sync", lambda sy, q=q: sy.dma_start(
                out=self.rowsT_h[q * 2048:(q + 1) * 2048, :]
                .rearrange("(m p) c -> p m c", p=128),
                in_=stg[:, 0:256].rearrange("p (m c) -> p m c", m=16)),
                waits=[e2])
        S.fence(["sync"])
        for q in range(6):
            for half in range(2):
                S.dma("sync", lambda sy, q=q, half=half: sy.dma_start(
                    out=self.fpair[half * 64:(half + 1) * 64,
                                   q * 256:(q + 1) * 256]
                    .rearrange("p (j r) -> p j r", j=32),
                    in_=self.rowsT_h[q * 2048:(q + 1) * 2048,
                                     half * 8:(half + 1) * 8]
                    .rearrange("(j p) r -> p j r", p=64)))
        self.PF.put(stg)
        self.fpair_ev = S.fence()
        self.tap("g_r", self.g_r[:, 0:128])
        self.tap("fpair", self.fpair[:, 0:256])

    def proj_pairs(self, wu, wev, pairs, store, eng_cycle=("vector", "scalar")):
        S = self.S
        ei = 0
        pevs = {}
        for p in pairs:
            pl = p % 2
            for tc in range(4):
                bi = (p * 4 + tc) % 6
                bk = self.pbank[bi]
                ev = None
                for d in range(8):
                    w = [wev] + (self.bank_free.get(bi, []) if d == 0 else [])
                    ev = S.emit("tensor", lambda tn, d=d, pl=pl, tc=tc, bk=bk, wu=wu:
                                tn.matmul(bk[:, 0:512],
                                          wu[:, d * 256 + pl * 128:
                                             d * 256 + (pl + 1) * 128],
                                          self.xnT[:, d * NT + tc * 512:
                                                   d * NT + (tc + 1) * 512],
                                          start=(d == 0), stop=(d == 7)),
                                waits=w, tick=(d == 7))
                eng = eng_cycle[ei % len(eng_cycle)]
                ei += 1
                if eng == "scalar":
                    e2 = S.emit("scalar", lambda sc, bk=bk, p=p, tc=tc, st=store:
                                sc.copy(st[p][:, tc * 512:(tc + 1) * 512],
                                        bk[:, 0:512]),
                                waits=[ev], tick=True)
                else:
                    e2 = S.emit(eng, lambda v, bk=bk, p=p, tc=tc, st=store:
                                v.tensor_copy(st[p][:, tc * 512:(tc + 1) * 512],
                                              bk[:, 0:512]),
                                waits=[ev], tick=True)
                self.bank_free[bi] = [e2]
                pevs.setdefault(p, []).append(e2)
        return pevs

    def l2norm_rows(self, tile, p, qidx, scale8, ready=()):
        S, PB = self.S, self.PB
        q2 = self.q2buf
        prev = getattr(self, "_l2prev", None) or []
        esq = S.emit("scalar", lambda s: s.activation(q2[:, :], tile[:, :], ACT.Square),
                     waits=list(prev) + list(ready), tick=True)
        evs = []
        for tc in range(4):
            bi = 4 + tc % 2
            bk = self.pbank[bi]
            ev = S.emit("tensor", lambda tn, tc=tc, bk=bk:
                        tn.matmul(bk[0:2, 0:512], self.hsel[:, :],
                                  q2[:, tc * 512:(tc + 1) * 512],
                                  start=True, stop=True),
                        waits=[esq] + self.bank_free.get(bi, []), tick=True)
            e2 = S.emit("vector", lambda v, bk=bk, tc=tc:
                        v.tensor_copy(self.l2s[0:2, tc * 512:(tc + 1) * 512],
                                      bk[0:2, 0:512]),
                        waits=[ev], tick=True)
            self.bank_free[bi] = [e2]
            evs.append(e2)
        sc, bi_ = (64.0, 64e-6) if scale8 else (1.0, 1e-6)
        e3 = S.emit("scalar", lambda s: s.activation(self.l2s[0:2, :], self.l2s[0:2, :],
                                                     ACT.Ln, bias=bi_, scale=sc),
                    waits=evs, tick=True)
        e4 = S.emit("scalar", lambda s: s.activation(self.l2s[0:2, :], self.l2s[0:2, :],
                                                     ACT.Exp, scale=-0.5),
                    waits=[e3], tick=True)
        ev = S.dma("gpsimd", lambda g, p=p, qidx=qidx:
                   g.dma_start(out=self.rrows_h[4 * p + qidx:4 * p + qidx + 3:2, :],
                               in_=self.l2s[0:2, :]), waits=[e4])
        self._l2prev = evs + [ev]
        return ev

    def rrep(self, unit, p, qidx, bf_src=True, qslot=None, waits=()):
        S = self.S
        ev = None
        for hh in range(2):
            if bf_src:
                r0 = (2 * p + hh) * 2 + qidx
                src = self.rrows_h[r0:r0 + 1, :]
                eng = "sync"
            else:
                r0 = (2 * p + hh) * 8 + qslot
                src = self.rows_h[r0:r0 + 1, :]
                eng = "gpsimd"
            ev = S.dma(eng, lambda e, hh=hh, src=src, unit=unit:
                       e.dma_start(out=unit[hh * 64:(hh + 1) * 64, :],
                                   in_=src.broadcast_to((64, NT))),
                       waits=waits)
        return ev

    def attn_sg(self, sg):
        S, PB, PF = self.S, self.PB, self.PF
        pairs = (2 * sg, 2 * sg + 1)
        fence0 = S.fence()
        sg0pre = False
        self.GMs = {}
        self.GM_ev = {}
        for p in pairs:
            GM = PF.get(f"GM{p}")
            ev = None
            for hh in range(2):
                r0 = (2 * p + hh) * 8 + 4
                ev = S.dma("scalar", lambda sy, hh=hh, GM=GM, r0=r0: sy.dma_start(
                    out=GM[hh * 64:(hh + 1) * 64, :],
                    in_=self.rows_h[r0:r0 + 1, :].broadcast_to((64, NT))))
            self.GMs[p] = GM
            self.GM_ev[p] = ev

        if sg0pre:
            qT, kT, vT = self._sg0
        else:
            qT, kT, gT, vT = {}, {}, {}, {}
            wus = {}
            wevs = {}
            prjev = {}
            for qty in "qkgv":
                wus[qty], wevs[qty] = self.load_wslice(self.w_hs[qty], sg, f"w{qty}")
            self.q2buf = PB.get("q2")
            for p in pairs:
                for qty, store in (("q", qT), ("k", kT), ("g", gT), ("v", vT)):
                    store[p] = PB.get(f"{qty}T{p}")
                    prjev[(p, qty)] = self.proj_pairs(
                        wus[qty], wevs[qty], (p,), store)[p]
                self.l2norm_rows(qT[p], p, 1, scale8=True, ready=prjev[(p, "q")])
                self.l2norm_rows(kT[p], p, 0, scale8=False, ready=prjev[(p, "k")])
                S.dma("sync", lambda sy, p=p: sy.dma_start(
                    out=self.stash_h[p * 128:(p + 1) * 128, :], in_=gT[p][:, :]),
                    waits=prjev[(p, "g")])
            S.fence()
            PB.put(self.q2buf)
            for qty in "qkgv":
                PB.put(wus[qty])
            for p in pairs:
                PB.put(gT[p])
        self.vT_sg = vT
        self._did_norms = not sg0pre

        if not self._did_norms:
            self.q2buf = PB.get("q2")
            for p in pairs:
                self.l2norm_rows(qT[p], p, 1, scale8=True)
                self.l2norm_rows(kT[p], p, 0, scale8=False)
            S.fence()
            PB.put(self.q2buf)
        rb = [PB.get("rb0"), PB.get("rb1")]
        rix = 0
        last_mult = {}
        for p in pairs:
            for qidx, tile in ((1, qT[p]), (0, kT[p])):
                u = rb[rix % 2]
                rix += 1
                guard = [last_mult[rix - 2]] if rix - 2 in last_mult else []
                ev = self.rrep(u, p, qidx, bf_src=True, waits=guard)
                w = [ev]
                self.tap(f"R{qidx}_{p}", u[:, 0:128])
                last_mult[rix] = S.emit(
                    "vector", lambda v, t=tile, u=u:
                    v.tensor_tensor(t[:, :], t[:, :], u[:, :], ALU.mult),
                    waits=w, tick=True)
        S.fence()
        PB.put(rb[0]); PB.put(rb[1])
        self.tap(f"qTn{sg}", qT[pairs[0]][:, 0:256])
        self.tap(f"kTn{sg}", kT[pairs[0]][:, 0:256])

        sgd = {}
        st = {}
        p0, p1 = pairs
        Ls, LTs, LTt = self.build_L(p0)
        U0, V0, a0 = self.p_attn(p0, kT[p0], qT[p0], Ls, LTs, LTt)
        S.fence()
        for t in (Ls, LTs, LTt):
            PB.put(t)
        Ya0, Yb0, kt0 = self.trans_y(p0, kT[p0])
        S.fence()
        PB.put(kT[p0]); PB.put(self.vT_sg[p0])
        Ls, LTs, LTt = self.build_L(p1)
        self.iteration(p0, U0, V0, Ya0, Yb0)
        U1, V1, a1 = self.p_attn(p1, kT[p1], qT[p1], Ls, LTs, LTt)
        S.fence()
        for t in (Ls, LTs, LTt):
            PB.put(t)
        Ya1, Yb1, kt1 = self.trans_y(p1, kT[p1])
        S.fence()
        PB.put(kT[p1]); PB.put(self.vT_sg[p1])
        sgd[p0] = self.prework(p0, U0, V0, Ya0, Yb0, a0, kt0, qT[p0])
        for t in (U0, V0, Ya0, Yb0, a0, kt0, qT[p0]):
            PB.put(t)
        self.iteration(p1, U1, V1, Ya1, Yb1)
        S.fence()
        sgd[p1] = self.prework(p1, U1, V1, Ya1, Yb1, a1, kt1, qT[p1])
        for t in (U1, V1, Ya1, Yb1, a1, kt1, qT[p1]):
            PB.put(t)
        S.fence()

        for p in pairs:
            QsT, M3T, Cst, OVst = sgd[p]
            for si, t in ((16, QsT), (24, M3T), (32, Cst), (40, OVst)):
                eng = "sync" if si % 16 == 0 else "scalar"
                S.dma(eng, lambda sy, si=si, p=p, t=t: sy.dma_start(
                    out=self.stash_h[(si + p) * 128:(si + p + 1) * 128, :],
                    in_=t[:, :]))
        S.fence(["sync", "scalar"])
        for p in pairs:
            for t in sgd[p]:
                PB.put(t)

    def scan_phase(self):
        S, PB = self.S, self.PB
        npair = 2 * OPT.get("nsg", NSG)
        M3 = {}
        Cs = {}
        lev = []
        for p in range(npair):
            M3[p] = PB.get(f"sM3{p}")
            Cs[p] = PB.get(f"sCs{p}")
            lev.append(S.dma("sync", lambda sy, p=p, t=M3[p]: sy.dma_start(
                out=t[:, :], in_=self.stash_h[(24 + p) * 128:(25 + p) * 128, :])))
            lev.append(S.dma("scalar", lambda sy, p=p, t=Cs[p]: sy.dma_start(
                out=t[:, :], in_=self.stash_h[(32 + p) * 128:(33 + p) * 128, :])))
        ev0 = S.emit("gpsimd", lambda g: g.memset(self.Sbf[:, :], 0.0), tick=True)
        self.scan_multi(list(range(npair)), M3, Cs, None, None, None,
                        sb_ev=[ev0] + lev[-2:])
        for p in range(npair):
            PB.put(M3[p]); PB.put(Cs[p])
        xev = self.exchange_start()
        self.gating()
        self.exchange_finish(xev)
        nb = max(1, npair // 4)
        for b in range(nb):
            bp = list(range(b * 4, b * 4 + 4))
            Qs = {}
            OV = {}
            lev = []
            for p in bp:
                M3[p] = PB.get(f"sM3{p}")
                Cs[p] = PB.get(f"sCs{p}")
                Qs[p] = PB.get(f"sQs{p}")
                OV[p] = PB.get(f"sOV{p}")
                for si, t in ((24, M3[p]), (32, Cs[p]), (16, Qs[p]), (40, OV[p])):
                    eng = "sync" if si in (24, 16) else "scalar"
                    lev.append(S.dma(eng, lambda sy, si=si, p=p, t=t: sy.dma_start(
                        out=t[:, :],
                        in_=self.stash_h[(si + p) * 128:(si + p + 1) * 128, :])))
            ocm = {p: PB.get(f"ocm{p}") for p in bp}
            self.scan_multi(bp, M3, Cs, Qs, OV, ocm,
                            sb_ev=lev[-2:] + [self._xfin])
            S.fence()
            for p in bp:
                PB.put(M3[p]); PB.put(Cs[p]); PB.put(Qs[p]); PB.put(OV[p])
            self.gtbuf = {}
            self.gt_ev = {}
            for p in bp:
                gt = PB.get(f"gt{p}")
                self.gtbuf[p] = gt
                self.gt_ev[p] = S.dma("sync", lambda sy, p=p, gt=gt: sy.dma_start(
                    out=gt[:, :], in_=self.stash_h[p * 128:(p + 1) * 128, :]))
            self.sqbuf = PB.get("osq")
            self._sq_prev = None
            for p in bp:
                self.opost(p, ocm[p])
                PB.put(ocm[p])
            PB.put(self.sqbuf)
            S.fence()

    def scan_multi(self, plist, M3, Cs, Qs, OV, ocm, sb_ev):
        S = self.S
        pass2 = Qs is not None
        evS = list(sb_ev)
        oces = []
        for j in range(NCH):
            bis = 4 + j % 2
            bks = self.pbank[bis]
            w0 = evS + self.bank_free.get(bis, [])
            ev = None
            if pass2:
                bio = 2 + j % 2
                bko = self.pbank[bio]
                w0 = w0 + self.bank_free.get(bio, [])
            n = len(plist)
            for ii, p in enumerate(plist):
                for hh in range(2):
                    hr = slice(64 * hh, 64 * hh + 64)
                    tp = (64 * hh, 64 * hh)
                    last = (ii == n - 1 and hh == 1)
                    ev = S.emit("tensor", lambda tn, hr=hr, tp=tp, p=p, j=j,
                                M3T=M3[p], bks=bks:
                                tn.matmul(bks[hr, p * 64 % 512:(p * 64 % 512) + 64],
                                          M3T[hr, j * C:(j + 1) * C],
                                          self.Sbf[hr, p * 64:(p + 1) * 64],
                                          start=True, stop=True, tile_position=tp),
                                waits=w0 if (ii == 0 and hh == 0) else [],
                                tick=(not pass2 and last))
                    if pass2:
                        ev = S.emit("tensor", lambda tn, hr=hr, tp=tp, p=p, j=j,
                                    QsT=Qs[p], bko=bko:
                                    tn.matmul(bko[hr, (p % 4) * 64:(p % 4) * 64 + 64],
                                              QsT[hr, j * C:(j + 1) * C],
                                              self.Sbf[hr, p * 64:(p + 1) * 64],
                                              start=True, stop=True,
                                              tile_position=tp),
                                    tick=last)
            sevs = []
            for ii, p in enumerate(plist):
                e = S.emit("vector", lambda v, p=p, j=j, Cst=Cs[p], bks=bks:
                           v.tensor_tensor(self.Sbf[:, p * 64:(p + 1) * 64],
                                           bks[:, p * 64 % 512:(p * 64 % 512) + 64],
                                           Cst[:, j * C:(j + 1) * C], ALU.add),
                           waits=[ev], tick=True)
                sevs.append(e)
                if pass2:
                    e2 = S.emit("scalar", lambda sc, p=p, j=j, bko=bko:
                                sc.copy(ocm[p][:, j * C:(j + 1) * C],
                                        bko[:, (p % 4) * 64:(p % 4) * 64 + 64]),
                                waits=[ev], tick=True)
                    if j >= NCH - 2:
                        oces.append(e2)
                    self.bank_free[bio] = [e2]
            evS = sevs
            self.bank_free[bis] = sevs
        if pass2:
            for p in plist:
                S.emit("vector", lambda v, p=p, OVst=OV[p]:
                       v.tensor_tensor(ocm[p][:, :], ocm[p][:, :], OVst[:, :],
                                       ALU.add),
                       waits=oces, tick=True)

    def exchange_start(self):
        S, PB = self.S, self.PB
        sf = S.fence()
        ev = S.dma("sync", lambda sy: sy.dma_start(out=self.ccin_h[:, :],
                                                   in_=self.Sbf[:, :]))
        cev = S.emit("gpsimd", lambda g: g.collective_compute(
            "AllGather", ALU.bypass,
            replica_groups=[list(range(NCORE))],
            ins=[self.ccin_h[:, :]], outs=[self.ccout_h[:, :]]),
            waits=[ev], tick=True)
        g0 = PB.get("gath0")
        g1 = PB.get("gath1")
        gev = []
        for gi, gu in ((0, g0), (1, g1)):
            gev.append(S.dma("sync", lambda sy, gi=gi, gu=gu: sy.dma_start(
                out=gu.rearrange("p (r c) -> p r c", r=4),
                in_=self.ccout_h.rearrange("(r p) c -> p r c", p=128)
                [:, gi * 4:(gi + 1) * 4, :]),
                waits=[cev]))
        return (g0, g1, gev)

    def exchange_finish(self, xev):
        S, PB = self.S, self.PB
        g0, g1, gev = xev
        e = S.emit("vector", lambda v: v.tensor_scalar(
            self.Sbf[:, :], g0[:, 0:512], self.recvm[:, 0:1], None, ALU.mult),
            waits=gev, tick=True)
        for r in range(1, 8):
            gu = g0 if r < 4 else g1
            e = S.emit("vector", lambda v, r=r, gu=gu: v.scalar_tensor_tensor(
                self.Sbf[:, :], gu[:, (r % 4) * 512:(r % 4 + 1) * 512],
                self.recvm[:, r:r + 1], self.Sbf[:, :], ALU.mult, ALU.add),
                waits=[e], tick=True)
        self._xfin = e
        PB.put(g0); PB.put(g1)

    def build_L(self, p):
        S, PB, PF = self.S, self.PB, self.PF
        GM = self.GMs[p]
        Dt = PF.get("Dt")
        ev = self.GM_ev[p]
        gD = S.emit("vector", lambda v: v.tensor_tensor(
            Dt.rearrange("p (j m) -> p j m", m=C),
            GM.rearrange("p (j m) -> p j m", m=C),
            bc(self.fp(4, p), NCH, C), ALU.subtract),
            waits=[ev, self.fpair_ev[-1] if self.fpair_ev else None], tick=True)
        e1 = S.emit("vector", lambda v: v.tensor_scalar_max(GM[:, :], Dt[:, :], 0.0),
                    waits=[gD], tick=True)
        e2 = S.emit("vector", lambda v: v.tensor_scalar_min(Dt[:, :], Dt[:, :], 0.0),
                    waits=[gD, e1], tick=True)
        Ls = PB.get(f"Ls{p}")
        LTt = PB.get(f"LTt{p}")
        LTs = PB.get(f"LTs{p}")
        e3 = S.emit("scalar", lambda s: s.activation(Ls[:, :], GM[:, :], ACT.Exp,
                                                     scale=-1.0),
                    waits=[e1], tick=True)
        e4 = S.emit("scalar", lambda s: s.activation(LTt[:, :], Dt[:, :], ACT.Exp),
                    waits=[e2], tick=True)
        m4 = lambda t: t.rearrange("p (o c) -> p o c", c=64)
        mb = lambda mk: mk.rearrange("p (o c) -> p o c", o=1).broadcast_to((128, 32, 64))
        e5 = S.emit("vector", lambda v: v.tensor_tensor(m4(LTs), m4(LTt),
                                                        mb(self.mustrict), ALU.mult),
                    waits=[e4], tick=True)
        e6 = S.emit("vector", lambda v: v.tensor_tensor(m4(LTt), m4(LTt),
                                                        mb(self.mutril), ALU.mult),
                    waits=[e5], tick=True)
        e7 = S.emit("vector", lambda v: v.tensor_tensor(m4(Ls), m4(Ls),
                                                        mb(self.mstrict), ALU.mult),
                    waits=[e3], tick=True)
        e8 = S.emit("vector", lambda v: v.tensor_tensor(
            Ls.rearrange("p (j m) -> p j m", m=C),
            Ls.rearrange("p (j m) -> p j m", m=C),
            bc(self.fp(0, p), NCH, C), ALU.mult), waits=[e7], tick=True)
        brep = PB.get("brep")
        ev = self.rrep(brep, p, 0, bf_src=False, qslot=0)
        e9 = S.emit("vector", lambda v: v.tensor_tensor(LTs[:, :], LTs[:, :],
                                                        brep[:, :], ALU.mult),
                    waits=[ev, e5], tick=True)
        self.L_evs = [e6, e8, e9]
        PF.put(GM); PF.put(Dt); PB.put(brep)
        return Ls, LTs, LTt

    def p_attn(self, p, kTn, qTn, Ls, LTs, LTt):
        S, PB = self.S, self.PB
        U = PB.get(f"U{p}")
        V = PB.get(f"V{p}")
        aT = PB.get(f"aT{p}")
        for jg in range(4):
            biP = 2 * (jg % 3)
            biA = biP + 1
            bkP, bkA = self.pbank[biP], self.pbank[biA]
            ev = None
            w0 = (self.bank_free.get(biP, []) + self.bank_free.get(biA, [])
                  + self.L_evs)
            for jj in range(8):
                j = jg * 8 + jj
                for hh in range(2):
                    hr = slice(64 * hh, 64 * hh + 64)
                    tp = (64 * hh, 64 * hh)
                    S.emit("tensor", lambda tn, hr=hr, tp=tp, j=j, jj=jj, bkP=bkP:
                           tn.matmul(bkP[hr, jj * 64:(jj + 1) * 64],
                                     kTn[hr, j * C:(j + 1) * C],
                                     kTn[hr, j * C:(j + 1) * C],
                                     start=True, stop=True, tile_position=tp),
                           waits=w0 if (jj == 0 and hh == 0) else [])
                    ev = S.emit("tensor", lambda tn, hr=hr, tp=tp, j=j, jj=jj, bkA=bkA:
                                tn.matmul(bkA[hr, jj * 64:(jj + 1) * 64],
                                          kTn[hr, j * C:(j + 1) * C],
                                          qTn[hr, j * C:(j + 1) * C],
                                          start=True, stop=True, tile_position=tp),
                                tick=(jj == 7 and hh == 1))
            sl = slice(jg * 512, (jg + 1) * 512)
            eV = S.emit("vector", lambda v, sl=sl, bkP=bkP:
                        v.tensor_tensor(V[:, sl], bkP[:, 0:512], Ls[:, sl], ALU.mult),
                        waits=[ev], tick=True)
            eU = S.emit("vector", lambda v, sl=sl, bkP=bkP:
                        v.tensor_tensor(U[:, sl], bkP[:, 0:512], LTs[:, sl], ALU.mult),
                        waits=[ev], tick=True)
            eA = S.emit("vector", lambda v, sl=sl, bkA=bkA:
                        v.tensor_tensor(aT[:, sl], bkA[:, 0:512], LTt[:, sl], ALU.mult),
                        waits=[ev], tick=True)
            self.bank_free[biP] = [eV, eU]
            self.bank_free[biA] = [eA]
        return U, V, aT

    def trans_y(self, p, kTn):
        S, PB = self.S, self.PB
        vTp = self.vT_sg[p]
        Ya = PB.get(f"Y{p}a")
        Yb = PB.get(f"Y{p}b")
        kt = PB.get(f"kt{p}")
        for jg in range(4):
            Yu = Ya if jg < 2 else Yb
            yv = Yu.rearrange("p (r b) -> p r b", b=128)
            for src, bkt_i, dsts in ((kTn, 0, "ku"), (vTp, 1, "v")):
                bkt = self.pbt[bkt_i]
                ev = None
                w0 = self.bt_free.get(bkt_i, [])
                for jj in range(8):
                    j = jg * 8 + jj
                    for hh in range(2):
                        hr = slice(64 * hh, 64 * hh + 64)
                        ev = S.emit("tensor",
                                    lambda tn, hr=hr, hh=hh, j=j, jj=jj, bkt=bkt, src=src:
                                    tn.transpose(bkt[hr, jj * 64:(jj + 1) * 64],
                                                 src[hr, j * C:(j + 1) * C],
                                                 self.idbf[hr, hh * 64:hh * 64 + 64],
                                                 tile_position=(64 * hh, 64 * hh)),
                                    waits=w0 if (jj == 0 and hh == 0) else [],
                                    tick=(jj == 7 and hh == 1))
                bv = bkt.rearrange("p (r b) -> p r b", b=64)[:, 0:8, :]
                r0 = (jg % 2) * 8
                if dsts == "ku":
                    e1 = S.emit("vector", lambda v, yv=yv, r0=r0, bv=bv, jg=jg:
                                v.tensor_tensor(yv[:, r0:r0 + 8, 64:128], bv,
                                                bc(self.fp(1, p, jg * 8, 8), 8, C),
                                                ALU.mult),
                                waits=[ev], tick=True)
                    e2 = S.emit("vector", lambda v, r0=r0, bv=bv, jg=jg:
                                v.tensor_tensor(kt.rearrange("p (r b) -> p r b", b=64)
                                                [:, jg * 8:jg * 8 + 8, :], bv,
                                                bc(self.fp(2, p, jg * 8, 8), 8, C),
                                                ALU.mult),
                                waits=[ev], tick=True)
                    self.bt_free[bkt_i] = [e1, e2]
                else:
                    e1 = S.emit("vector", lambda v, yv=yv, r0=r0, bv=bv, jg=jg:
                                v.tensor_tensor(yv[:, r0:r0 + 8, 0:64], bv,
                                                bc(self.fp(0, p, jg * 8, 8), 8, C),
                                                ALU.mult),
                                waits=[ev], tick=True)
                    self.bt_free[bkt_i] = [e1]
        return Ya, Yb, kt

    def iteration(self, p, U, V, Ya, Yb):
        S = self.S
        evY, evU, evV = {}, {}, {}
        for s in range(OPT.get('iter_rounds', 6)):
            for jg in range(8):
                Yu = Ya if jg < 4 else Yb
                bia = 2 * (jg % 3)
                bis = bia + 1
                bka, bks = self.pbank[bia], self.pbank[bis]
                w0 = (self.bank_free.get(bia, []) + self.bank_free.get(bis, [])
                      + [evY.get(jg), evU.get(jg), evV.get(jg)])
                ev = None
                for jj in range(4):
                    j = jg * 4 + jj
                    jr = j % 16
                    for hh in range(2):
                        hr = slice(64 * hh, 64 * hh + 64)
                        tp = (64 * hh, 64 * hh)
                        S.emit("tensor", lambda tn, hr=hr, tp=tp, j=j, jj=jj,
                               bka=bka, Yu=Yu, jr=jr:
                               tn.matmul(bka[hr, jj * 128:(jj + 1) * 128],
                                         U[hr, j * C:(j + 1) * C],
                                         Yu[hr, jr * 128:(jr + 1) * 128],
                                         start=True, stop=True, tile_position=tp),
                               waits=w0 if (jj == 0 and hh == 0) else [])
                        if s < 5:
                            S.emit("tensor", lambda tn, hr=hr, tp=tp, j=j, jj=jj, bks=bks:
                                   tn.matmul(bks[hr, jj * 64:(jj + 1) * 64],
                                             V[hr, j * C:(j + 1) * C],
                                             U[hr, j * C:(j + 1) * C],
                                             start=True, stop=True, tile_position=tp))
                        if s < 4:
                            S.emit("tensor", lambda tn, hr=hr, tp=tp, j=j, jj=jj, bks=bks:
                                   tn.matmul(bks[hr, 256 + jj * 64:256 + (jj + 1) * 64],
                                             U[hr, j * C:(j + 1) * C],
                                             V[hr, j * C:(j + 1) * C],
                                             start=True, stop=True, tile_position=tp))
                ev = S.tick_last("tensor")
                yl = Yu.rearrange("p (r b) -> p r b", b=128)
                r0 = (jg % 4) * 4
                bkav = bka.rearrange("p (r b) -> p r b", b=128)
                evY[jg] = S.emit("vector", lambda v, yl=yl, r0=r0, bkav=bkav:
                                 v.tensor_tensor(yl[:, r0:r0 + 4, :],
                                                 bkav, yl[:, r0:r0 + 4, :], ALU.add),
                                 waits=[ev], tick=True)
                frees = [evY[jg]]
                if s < 5:
                    evU[jg] = S.emit("scalar", lambda sc, jg=jg, bks=bks:
                                     sc.copy(U[:, jg * 256:(jg + 1) * 256],
                                             bks[:, 0:256]),
                                     waits=[ev], tick=True)
                    frees.append(evU[jg])
                if s < 4:
                    evV[jg] = S.emit("scalar", lambda sc, jg=jg, bks=bks:
                                     sc.copy(V[:, jg * 256:(jg + 1) * 256],
                                             bks[:, 256:512]),
                                     waits=[ev], tick=True)
                    frees.append(evV[jg])
                self.bank_free[bia] = [evY[jg]]
                self.bank_free[bis] = frees[1:] or [evY[jg]]

    def prework(self, p, U, V, Ya, Yb, aT, kt, qTn):
        S, PB = self.S, self.PB
        QsT = PB.get(f"Qs{p}")
        M3T = PB.get(f"M3{p}")
        Cst = PB.get(f"Cs{p}")
        OVst = PB.get(f"OV{p}")
        EG = PB.get("EGrep")
        egv = self.rrep(EG, p, 5, bf_src=False, qslot=5)
        et = self.etb
        et2 = self.qtb
        e_qprev = getattr(self, "_pw_eq", None)
        e_iprev = getattr(self, "_pw_ei", None)
        for jg in range(4):
            sl4 = slice(jg * 512, (jg + 1) * 512)
            e_q = S.emit("vector", lambda v, sl4=sl4: v.tensor_tensor(
                et2[:, :], qTn[:, sl4], EG[:, sl4], ALU.mult),
                waits=[egv, e_qprev], tick=True)
            e_i = S.emit("vector", lambda v, jg=jg: v.tensor_tensor(
                et.rearrange("p (r b) -> p r b", b=64),
                self.imaskr.rearrange("p (o b) -> p o b", o=1)
                .broadcast_to((128, 8, 64)),
                bc(self.fp(3, p, jg * 8, 8), 8, C), ALU.mult),
                waits=[e_iprev], tick=True)
            bks = [self.pbank[i] for i in range(4)]
            ev = None
            w0 = sum((self.bank_free.get(i, []) for i in range(4)), [])
            for jj in range(8):
                j = jg * 8 + jj
                jr = j % 16
                Yu = Ya if j < 16 else Yb
                for hh in range(2):
                    hr = slice(64 * hh, 64 * hh + 64)
                    tp = (64 * hh, 64 * hh)
                    kcd = Yu[hr, jr * 128 + 64:jr * 128 + 128]
                    vp = Yu[hr, jr * 128:jr * 128 + 64]
                    js = slice(jj * 64, (jj + 1) * 64)
                    S.emit("tensor", lambda tn, hr=hr, tp=tp, j=j, js=js,
                           kcd=kcd, b=bks[0]:
                           tn.matmul(b[hr, js], kcd, aT[hr, j * C:(j + 1) * C],
                                     start=True, stop=True, tile_position=tp),
                           waits=w0 if (jj == 0 and hh == 0) else [])
                    S.emit("tensor", lambda tn, hr=hr, tp=tp, j=j, js=js,
                           kcd=kcd, b=bks[1]:
                           tn.matmul(b[hr, js], kcd, kt[hr, j * C:(j + 1) * C],
                                     start=True, stop=True, tile_position=tp))
                    S.emit("tensor", lambda tn, hr=hr, tp=tp, j=j, js=js,
                           vp=vp, b=bks[2]:
                           tn.matmul(b[hr, js], kt[hr, j * C:(j + 1) * C], vp,
                                     start=True, stop=True, tile_position=tp))
                    S.emit("tensor", lambda tn, hr=hr, tp=tp, j=j, js=js,
                           vp=vp, b=bks[3]:
                           tn.matmul(b[hr, js], aT[hr, j * C:(j + 1) * C], vp,
                                     start=True, stop=True, tile_position=tp))
            ev = S.tick_last("tensor")
            sl = slice(jg * 512, (jg + 1) * 512)
            e0 = S.emit("vector", lambda v, sl=sl, b=bks[0]:
                        v.scalar_tensor_tensor(QsT[:, sl], b[:, 0:512], -1.0,
                                               et2[:, :], ALU.mult, ALU.add),
                        waits=[ev, e_q], tick=True)
            e1 = S.emit("vector", lambda v, sl=sl, b=bks[1]:
                        v.scalar_tensor_tensor(M3T[:, sl], b[:, 0:512], -1.0,
                                               et[:, :], ALU.mult, ALU.add),
                        waits=[ev, e_i], tick=True)
            e_qprev, e_iprev = e0, e1
            e2 = S.emit("scalar", lambda s, sl=sl, b=bks[2]:
                        s.copy(Cst[:, sl], b[:, 0:512]), waits=[ev], tick=True)
            e3 = S.emit("scalar", lambda s, sl=sl, b=bks[3]:
                        s.copy(OVst[:, sl], b[:, 0:512]), waits=[ev], tick=True)
            for i, e in enumerate((e0, e1, e2, e3)):
                self.bank_free[i] = [e]
        self._pw_eq, self._pw_ei = e_qprev, e_iprev
        S.fence()
        PB.put(EG)
        return QsT, M3T, Cst, OVst

    def opost(self, p, oc):
        S, PB = self.S, self.PB
        sq = self.sqbuf
        e1 = S.emit("scalar", lambda s: s.activation(sq[:, :], oc[:, :], ACT.Square),
                    waits=[getattr(self, "_sq_prev", None)], tick=True)
        e2 = S.emit("vector", lambda v: v.tensor_reduce(
            self.rmsq[:, 0:32], sq.rearrange("p (j m) -> p j m", m=C), AX.X, ALU.add),
            waits=[e1, getattr(self, "_opost_prev", None),
                   getattr(self, "_gate_done", None)], tick=True)
        self._sq_prev = e2
        e3 = S.emit("scalar", lambda s: s.activation(self.rmsq[:, 32:64],
                                                     self.rmsq[:, 0:32], ACT.Ln,
                                                     scale=1.0 / 64, bias=1e-5),
                    waits=[e2], tick=True)
        e4 = S.emit("scalar", lambda s: s.activation(self.rmsq[:, 0:32],
                                                     self.rmsq[:, 32:64], ACT.Exp,
                                                     scale=-0.5),
                    waits=[e3], tick=True)
        e5 = S.emit("vector", lambda v: v.tensor_tensor(
            oc.rearrange("p (j m) -> p j m", m=C),
            oc.rearrange("p (j m) -> p j m", m=C),
            bc(self.rmsq[:, 0:32], 32, C), ALU.mult), waits=[e4], tick=True)
        self._opost_prev = e5
        self.tap(f"orms{p}", oc[:, 0:128])
        gt = self.gtbuf[p]
        e6 = S.emit("scalar", lambda s: s.activation(gt[:, :], gt[:, :], ACT.Silu),
                    waits=[self.gt_ev[p]], tick=True)
        ofin = PB.get(f"ofin{p}")
        e7s = []
        for jg in range(4):
            bkt = self.pbt[jg % 2]
            ev = None
            w0 = self.bt_free.get(jg % 2, []) + [e5]
            for jj in range(8):
                j = jg * 8 + jj
                for hh in range(2):
                    hr = slice(64 * hh, 64 * hh + 64)
                    ev = S.emit("tensor", lambda tn, hr=hr, hh=hh, j=j, jj=jj, bkt=bkt:
                                tn.transpose(bkt[hr, jj * 64:(jj + 1) * 64],
                                             oc[hr, j * C:(j + 1) * C],
                                             self.idbf[hr, hh * 64:hh * 64 + 64],
                                             tile_position=(64 * hh, 64 * hh)),
                                waits=w0 if (jj == 0 and hh == 0) else [],
                                tick=(jj == 7 and hh == 1))
            e7 = S.emit("vector", lambda v, jg=jg, bkt=bkt:
                        v.tensor_scalar(ofin[:, jg * 512:(jg + 1) * 512],
                                        bkt[:, 0:512], self.normw[:, :], None,
                                        ALU.mult),
                        waits=[ev], tick=True)
            e7s.append(e7)
            self.bt_free[jg % 2] = [e7]
        self.tap(f"ofpre{p}", ofin[:, 0:128])
        self.tap(f"gt{p}", gt[:, 0:128])
        e8 = S.emit("vector", lambda v: v.tensor_tensor(ofin[:, :], ofin[:, :],
                                                        gt[:, :], ALU.mult),
                    waits=[e6] + e7s, tick=True)
        sev = S.dma("sync", lambda sy: sy.dma_start(out=self.stash_h[(8 + p) * 128:(9 + p) * 128, :],
                                                    in_=ofin[:, :]), waits=[e8])
        S.fence()
        PB.put(gt)
        PB.put(ofin)
        self.tap(f"ofin{p}", ofin[:, 0:256])

    def wo_prefetch(self):
        S, PB = self.S, self.PB
        of = {}
        for p in range(8):
            of[p] = PB.get(f"wof{p}")
            S.dma("sync", lambda sy, p=p, of=of: sy.dma_start(
                out=of[p][:, :],
                in_=self.stash_h[(8 + p) * 128:(9 + p) * 128, :]))
        wu, wev = self.load_w1024(self.w_hs["o"], "wo")
        self._wo_pre = (of, wu, wev)

    def wo_phase(self, half):
        S, PB, PF = self.S, self.PB, self.PF
        if half == 1:
            self.wo_prefetch()
        of, wu, wev = self._wo_pre
        S.fence(["sync", "gpsimd"])
        if half == 0:
            self.ACC = [PF.get(f"acc{i}") for i in range(4)]
        ei = 0
        for tl in range(8):
            t = half * 8 + tl
            for och in range(2):
                bi = (2 * tl + och) % 6
                bk = self.pbank[bi]
                ev = None
                for p in range(8):
                    w = (self.bank_free.get(bi, []) + [wev]) if p == 0 else []
                    ev = S.emit("tensor", lambda tn, p=p, t=t, och=och, bk=bk, wu=wu:
                                tn.matmul(bk[:, 0:512],
                                          of[p][:, t * 128:(t + 1) * 128],
                                          self.wsl(wu, p, och, 512),
                                          start=(p == 0), stop=(p == 7)),
                                waits=w, tick=(p == 7))
                au = self.ACC[tl // 2]
                asl = slice((tl % 2) * 1024 + och * 512,
                            (tl % 2) * 1024 + (och + 1) * 512)
                if ei % 2 == 0:
                    e2 = S.emit("vector", lambda v, au=au, asl=asl, bk=bk:
                                v.tensor_copy(au[:, asl], bk[:, 0:512]),
                                waits=[ev], tick=True)
                else:
                    e2 = S.emit("scalar", lambda sc, au=au, asl=asl, bk=bk:
                                sc.copy(au[:, asl], bk[:, 0:512]),
                                waits=[ev], tick=True)
                ei += 1
                self.bank_free[bi] = [e2]
        S.fence()
        for p in range(8):
            PB.put(of[p])
        for u in wu:
            PB.put(u)

    def gating(self):
        S, PB = self.S, self.PB
        for tc in range(4):
            bi = 4 + tc % 2
            bk = self.pbank[bi]
            ev = None
            for d in range(8):
                ev = S.emit("tensor", lambda tn, d=d, tc=tc, bk=bk:
                            tn.matmul(bk[0:8, 0:512],
                                      self.wgfsb[:, d * 8:(d + 1) * 8],
                                      self.xnT[:, d * NT + tc * 512:
                                               d * NT + (tc + 1) * 512],
                                      start=(d == 0), stop=(d == 7)),
                            waits=self.bank_free.get(bi, []) if d == 0 else [],
                            tick=(d == 7))
            e2 = S.emit("vector", lambda v, bk=bk, tc=tc:
                        v.tensor_scalar(self.logit[0:8, tc * 512:(tc + 1) * 512],
                                        bk[0:8, 0:512], self.bgfsb[:, :], None,
                                        ALU.add),
                        waits=[ev], tick=True)
            self.bank_free[bi] = [e2]
        S.fence()
        bk = self.pbank[0]
        ev = None
        for t in range(16):
            ev = S.emit("tensor", lambda tn, t=t, bk=bk:
                        tn.transpose(bk[:, t * 8:(t + 1) * 8],
                                     self.logit[0:8, t * 128:(t + 1) * 128],
                                     self.idf[0:8, 0:8]),
                        waits=self.bank_free.get(0, []) if t == 0 else [],
                        tick=(t == 15))
        e = S.emit("vector", lambda v: v.tensor_copy(self.wtok[:, :], bk[:, 0:128]),
                   waits=[ev], tick=True)
        self.bank_free[0] = [e]
        wv = self.wtok.rearrange("p (t e) -> p t e", t=16)
        sc = self.wsc
        av = sc.rearrange("p (t e) -> p t e", t=16)
        m1 = self.rmsq[:, 0:16]
        m2 = self.rmsq[:, 16:32]
        den = self.rmsq[:, 32:48]
        e1 = S.emit("vector", lambda v: v.reduce_max(m1, wv, AX.X), waits=[e], tick=True)
        e2 = S.emit("vector", lambda v: v.tensor_tensor(av, wv, bc(m1, 16, 8),
                                                        ALU.is_ge), waits=[e1], tick=True)
        e3 = S.emit("vector", lambda v: v.scalar_tensor_tensor(
            sc[:, :], sc[:, :], -1e30, self.wtok[:, :], ALU.mult, ALU.add),
            waits=[e2], tick=True)
        e4 = S.emit("vector", lambda v: v.reduce_max(m2, av, AX.X), waits=[e3], tick=True)
        e5 = S.emit("vector", lambda v: v.tensor_tensor(av, wv, bc(m2, 16, 8),
                                                        ALU.is_ge), waits=[e4], tick=True)
        e6 = S.emit("vector", lambda v: v.tensor_tensor(den, m2, m1, ALU.subtract),
                    waits=[e4], tick=True)
        e7 = S.emit("scalar", lambda s: s.activation(den, den, ACT.Exp),
                    waits=[e6], tick=True)
        e8 = S.emit("vector", lambda v: v.tensor_scalar_add(den, den, 1.0),
                    waits=[e7], tick=True)
        e9 = S.emit("vector", lambda v: v.reciprocal(den, den), waits=[e8], tick=True)
        e10 = S.emit("vector", lambda v: v.tensor_tensor(wv, wv, bc(m1, 16, 8),
                                                         ALU.subtract),
                     waits=[e5], tick=True)
        e11 = S.emit("scalar", lambda s: s.activation(self.wtok[:, :], self.wtok[:, :],
                                                      ACT.Exp), waits=[e10], tick=True)
        e12 = S.emit("vector", lambda v: v.tensor_tensor(wv, wv, av, ALU.mult),
                     waits=[e11], tick=True)
        e13 = S.emit("vector", lambda v: v.tensor_tensor(wv, wv, bc(den, 16, 8),
                                                         ALU.mult),
                     waits=[e9, e12], tick=True)
        self._gate_done = e13
        self.tap("wtok", self.wtok[:, :])
        self.wTb = PB.get("wTb")
        wTb = self.wTb
        for tg in range(4):
            bk2 = self.pbank[1]
            ev = None
            for tt in range(4):
                t = tg * 4 + tt
                ev = S.emit("tensor", lambda tn, t=t, tt=tt, bk2=bk2:
                            tn.transpose(bk2[0:8, tt * 128:(tt + 1) * 128],
                                         self.wtok[:, t * 8:(t + 1) * 8],
                                         self.idf[:, :]),
                            waits=(self.bank_free.get(1, []) + [e13])
                            if tt == 0 else [],
                            tick=(tt == 3))
            e = S.emit("vector", lambda v, tg=tg, bk2=bk2, wTb=wTb:
                       v.tensor_copy(wTb[0:8, tg * 512:(tg + 1) * 512],
                                     bk2[0:8, 0:512]),
                       waits=[ev], tick=True)
            self.bank_free[1] = [e]
        S.fence()

    def ffn_phase(self, half):
        S, PB = self.S, self.PB
        ei = 0
        for tl in range(8):
            t = half * 8 + tl
            for och in range(2):
                bi = (2 * tl + och) % 6
                bk = self.pbank[bi]
                ev = S.emit("tensor", lambda tn, t=t, och=och, bk=bk:
                            tn.matmul(bk[:, 0:512],
                                      self.wTb[0:8, t * 128:(t + 1) * 128],
                                      self.besb[0:8, och * 512:(och + 1) * 512],
                                      start=True, stop=True),
                            waits=self.bank_free.get(bi, []) + [self.consts_g],
                            tick=True)
                ei += 1
                au = self.ACC[tl // 2]
                asl = slice((tl % 2) * 1024 + och * 512,
                            (tl % 2) * 1024 + (och + 1) * 512)
                e2 = S.emit("vector", lambda v, au=au, asl=asl, bk=bk:
                            v.tensor_tensor(au[:, asl], bk[:, 0:512], au[:, asl],
                                            ALU.add),
                            waits=[ev], tick=True)
                self.bank_free[bi] = [e2]
        S.fence()
        eu = {}
        ldev = {}
        for e2i in range(2):
            eu[e2i], ldev[e2i] = self._load_we(e2i)
        for ep in range(4):
            e0, e1 = 2 * ep, 2 * ep + 1
            u0, u1 = eu[e0], eu[e1]
            lev = [ldev[e0], ldev[e1]]
            if ep < 3:
                nxt0, nxt1 = 2 * ep + 2, 2 * ep + 3
                eu[nxt0], ldev[nxt0] = self._load_we(nxt0)
                eu[nxt1], ldev[nxt1] = self._load_we(nxt1)
            for tl in range(8):
                t = half * 8 + tl
                b0 = (tl * 4) % 6
                bidx = [(b0 + i) % 6 for i in range(4)]
                bks = [self.pbank[i] for i in bidx]
                ev = None
                w0 = sum((self.bank_free.get(i, []) for i in bidx), []) + lev
                for d in range(8):
                    for e2i, uu in ((0, u0), (1, u1)):
                        for och in range(2):
                            ev = S.emit("tensor", lambda tn, d=d, t=t, uu=uu,
                                        och=och, bk=bks[e2i * 2 + och]:
                                        tn.matmul(bk[:, 0:512],
                                                  self.xnT[:, d * NT + t * 128:
                                                           d * NT + (t + 1) * 128],
                                                  uu[d // 2][:, (d % 2) * 1024
                                                             + och * 512:
                                                             (d % 2) * 1024
                                                             + (och + 1) * 512],
                                                  start=(d == 0), stop=(d == 7)),
                                        waits=w0 if (d == 0 and e2i == 0 and och == 0)
                                        else [],
                                        tick=(d == 7 and e2i == 1 and och == 1))
                for e2i, ee in ((0, e0), (1, e1)):
                    for och in range(2):
                        au = self.ACC[tl // 2]
                        asl = slice((tl % 2) * 1024 + och * 512,
                                    (tl % 2) * 1024 + (och + 1) * 512)
                        e3 = S.emit("vector", lambda v, au=au, asl=asl, ee=ee, t=t,
                                    bk=bks[e2i * 2 + och]:
                                    v.scalar_tensor_tensor(
                                        au[:, asl], bk[:, 0:512],
                                        self.wtok[:, t * 8 + ee:t * 8 + ee + 1],
                                        au[:, asl], ALU.mult, ALU.add),
                                    waits=[ev], tick=True)
                        self.bank_free[bidx[e2i * 2 + och]] = [e3]
            for u in u0 + u1:
                PB.put(u)
        if half == 1:
            PB.put(self.wTb)

    def _load_we(self, e, waits=()):
        units = [self.PB.get(f"we{e}_{i}") for i in range(4)]
        ev = None
        for u in range(4):
            ev = self.S.dma("gpsimd", lambda g, un=units[u], u=u, e=e:
                            g.dma_start(
                                out=un.rearrange("p (a c) -> p a c", a=2),
                                in_=self.we_h[e, u * 256:(u + 1) * 256, :]
                                .rearrange("(a p) c -> p a c", p=128)),
                            waits=waits)
        return units, ev

    def out_half(self, half):
        S = self.S
        S.fence()
        for a in range(4):
            r0 = half * 1024 + a * 256
            S.dma("gpsimd", lambda g, a=a, r0=r0: g.dma_start(
                out=self.out_h[r0:r0 + 256, :]
                .rearrange("(i p) c -> p i c", p=128),
                in_=self.ACC[a].rearrange("p (i c) -> p i c", i=2),
                accum_op=ALU.add),
                waits=[self.xcopy_ev])
        S.fence(["gpsimd"])

    def finish(self):
        S = self.S
        S.fence()
        S.dma("sync", lambda sy: sy.dma_start(out=self.dbg_h[:, :],
                                              in_=self.dbgsb[:, :]),
              waits=[self.xcopy_ev])
        S.fence()
        final = []
        for e in ("sync", "gpsimd", "scalar"):
            if S.dcnt[e]:
                final.append((e + "_d", S.dcnt[e]))
        for e in ENGS:
            S.pending[e].extend(final)
        with self.nc.Block() as block:
            def runner(name):
                def body(eng):
                    for oi, (waits, fn, ticks, _d) in enumerate(S.ops[name]):
                        try:
                            for sk, val in waits:
                                eng.wait_ge(self.sems[sk], val)
                            inst = fn(eng)
                            for sk, amt in ticks:
                                assert inst is not None
                                inst = inst.then_inc(self.sems[sk], amt)
                        except Exception as ex:
                            raise RuntimeError(f"emit fail {name} op {oi}: {ex}") from ex
                    seen = {}
                    for sk, val in S.pending[name]:
                        if val > seen.get(sk, -1):
                            seen[sk] = val
                    for sk, val in seen.items():
                        if val > S.last_wait[name].get(sk, -1):
                            eng.wait_ge(self.sems[sk], val)
                return body
            block.tensor(runner("tensor"))
            block.vector(runner("vector"))
            block.scalar(runner("scalar"))
            block.gpsimd(runner("gpsimd"))
            block.sync(runner("sync"))
        self.ctx.close()
        return self.nc


def build_nc():
    return Builder().build()


def _consts():
    idbf = np.eye(128, dtype=BFNP)
    idf = np.eye(128, dtype=np.float32)
    hsel = np.zeros((128, 2), BFNP)
    hsel[0:64, 0] = 1
    hsel[64:128, 1] = 1
    p = np.arange(128)[:, None] % 64
    m = np.arange(64)[None, :]
    return dict(
        idbf=idbf, idf=idf, hsel=hsel,
        mstrict=(p > m).astype(BFNP),
        mustrict=(m > p).astype(BFNP),
        mutril=(m >= p).astype(BFNP),
        imaskr=(m == p).astype(BFNP),
    )


def make_in_maps(inputs):
    x = np.ascontiguousarray(np.asarray(inputs["L0_attn_x"], np.float32))
    consts = _consts()
    in_maps = []
    for c in range(NCORE):
        b, half = c // 2, c % 2
        recvm = np.zeros((128, 8), np.float32)
        if half == 1:
            recvm[:, c - 1] = 1.0
        m = dict(
            x=x[b, half * NT:(half + 1) * NT],
            wq=inputs["L0_attn_Wq"], wk=inputs["L0_attn_Wk"],
            wv=inputs["L0_attn_Wv"], wg=inputs["L0_attn_Wg"],
            wo=inputs["L0_attn_Wo"], wb=inputs["L0_attn_Wb"],
            wa=inputs["L0_attn_Wa"],
            dtb=np.asarray(inputs["L0_attn_dt_bias"], np.float32)[:, None],
            alog=np.asarray(inputs["L0_attn_A_log"], np.float32)[:, None],
            normw=np.tile(np.asarray(inputs["L0_attn_norm_w"], np.float32), 2)[:, None],
            we=inputs["L0_ffn_We"], be=inputs["L0_ffn_be"],
            wgf=inputs["L0_ffn_Wg"],
            bgf=np.asarray(inputs["L0_ffn_bg"], np.float32)[:, None],
            recvm=recvm, **consts,
        )
        in_maps.append({k: np.ascontiguousarray(np.asarray(v)) for k, v in m.items()})
    return in_maps


def kernel(**inputs):
    in_maps = make_in_maps(inputs)
    nc = build_nc()
    res = run_bass_kernel_spmd(nc, in_maps, core_ids=list(range(NCORE)))
    out = np.zeros((B, T, D), np.float32)
    for c in range(NCORE):
        b, half = c // 2, c % 2
        out[b, half * NT:(half + 1) * NT] = res.results[c]["out"]
    return out


# version 18 (final)
# speedup vs baseline: 1.0242x; 1.0242x over previous
import contextlib
import numpy as np
import ml_dtypes
import concourse.bass as bass
import concourse.mybir as mybir
from concourse.bass_utils import run_bass_kernel_spmd

F32 = mybir.dt.float32
BF16 = mybir.dt.bfloat16
ALU = mybir.AluOpType
ACT = mybir.ActivationFunctionType
AX = mybir.AxisListType
BFNP = ml_dtypes.bfloat16

B, T, D = 4, 4096, 1024
H, K, C = 16, 64, 64
NT = 2048
NCH = 32
E = 8
NSG = 4
NCORE = 8

ENGS = ("tensor", "vector", "scalar", "gpsimd", "sync")
OPT = {"upto": "all", "taps": ()}


class Sched:
    def __init__(self):
        self.ops = {e: [] for e in ENGS}
        self.cnt = {e: 0 for e in ENGS}
        self.dcnt = {"sync": 0, "gpsimd": 0, "scalar": 0}
        self.last_wait = {e: {} for e in ENGS}
        self.pending = {e: [] for e in ENGS}
        self.dirty = False

    def emit(self, eng, fn, waits=(), tick=False):
        w = list(self.pending[eng]) + [x for x in waits if x is not None]
        self.pending[eng] = []
        dedup = []
        for sk, val in w:
            if val > self.last_wait[eng].get(sk, -1):
                dedup.append((sk, val))
                self.last_wait[eng][sk] = val
        entry = [dedup, fn, [], False]
        self.ops[eng].append(entry)
        self.dirty = True
        if tick:
            self.cnt[eng] += 1
            entry[2].append((eng + "_c", 1))
            return (eng + "_c", self.cnt[eng])
        return None

    def dma(self, eng, fn, waits=()):
        self.emit(eng, fn, waits=waits)
        self.dcnt[eng] += 16
        self.ops[eng][-1][2].append((eng + "_d", 16))
        self.ops[eng][-1][3] = True
        return (eng + "_d", self.dcnt[eng])

    def tick_last(self, eng):
        if not self.ops[eng]:
            return None
        entry = self.ops[eng][-1]
        if entry[3]:
            return None
        for sk, _ in entry[2]:
            if sk == eng + "_c":
                return (eng + "_c", self.cnt[eng])
        self.cnt[eng] += 1
        entry[2].append((eng + "_c", 1))
        return (eng + "_c", self.cnt[eng])

    def fence(self, engs=None, to=None):
        if not self.dirty and engs is None and to is None:
            return []
        if engs is None and to is None:
            self.dirty = False
        evs = []
        for e in (engs or ENGS):
            if e in self.dcnt and self.dcnt[e]:
                evs.append((e + "_d", self.dcnt[e]))
            ev = self.tick_last(e)
            if ev is not None:
                evs.append(ev)
        for e in (to or ENGS):
            self.pending[e].extend(evs)
        return evs


class Pool:
    def __init__(self, tensors, name):
        self.t = list(tensors)
        self.free = list(range(len(self.t)))
        self.used = {}
        self.name = name
        self.hwm = 0
        self.sched = None

    def get(self, tag):
        assert self.free, (f"pool {self.name} empty for {tag}; "
                           f"used={sorted(self.used.values())}")
        i = self.free.pop()
        self.used[i] = tag
        self.hwm = max(self.hwm, len(self.used))
        return self.t[i]

    def put(self, t):
        if self.sched is not None:
            self.sched.fence()
        for i, tt in enumerate(self.t):
            if tt is t:
                del self.used[i]
                self.free.append(i)
                return
        raise KeyError("not a pool tensor")


def bc(ap2d, n, reps):
    p = ap2d.shape[0]
    return ap2d.rearrange("p (n o) -> p n o", o=1).broadcast_to((p, n, reps))


class Builder:
    def __init__(self):
        self.nc = bass.Bass("TRN2", target_bir_lowering=False, debug=False)
        self.S = Sched()
        self.ctx = contextlib.ExitStack()
        self.dbg_col = 0
        self.taps = list(OPT["taps"])
        self.bank_free = {}
        self.bt_free = {}
        self._m3cs_held = {}
        self._ofin_held = {}

    def sb(self, n, s, dt):
        return self.ctx.enter_context(self.nc.sbuf_tensor(n, list(s), dt))

    def ps(self, n, s, dt):
        return self.ctx.enter_context(self.nc.psum_tensor(n, list(s), dt))

    def tap(self, name, ap):
        if name not in self.taps or self.dbg_col >= 512:
            return
        w = min(int(np.prod(ap.shape[1:])), 512 - self.dbg_col)
        c0 = self.dbg_col
        self.dbg_col += w
        rows = ap.shape[0]
        if len(ap.shape) != 2 or ap.shape[1] != w:
            if len(ap.shape) == 2:
                ap = ap[:, 0:w]
            else:
                return
        self.S.fence()
        self.S.emit("vector", lambda v, ap=ap, c0=c0, w=w, rows=rows:
                    v.tensor_copy(self.dbgsb[0:rows, c0:c0 + w], ap), tick=True)
        self.S.fence()

    def declare(self):
        nc = self.nc
        din = lambda name, shape, dt=F32: nc.dram_tensor(
            name, list(shape), dt, kind="ExternalInput").ap()
        self.x_h = din("x", (NT, D))
        self.w_hs = {q: din("w" + q, (D, D)) for q in "qkvgo"}
        self.wb_h = din("wb", (D, 16)); self.wa_h = din("wa", (D, 16))
        self.dtb_h = din("dtb", (16, 1)); self.alog_h = din("alog", (16, 1))
        self.normw_h = din("normw", (128, 1))
        self.we_h = din("we", (E, D, D)); self.be_h = din("be", (E, D))
        self.wgf_h = din("wgf", (D, E)); self.bgf_h = din("bgf", (8, 1))
        self.idbf_h = din("idbf", (128, 128), BF16)
        self.idf_h = din("idf", (128, 128), F32)
        self.hsel_h = din("hsel", (128, 2), BF16)
        self.mstrict_h = din("mstrict", (128, 64), BF16)
        self.mustrict_h = din("mustrict", (128, 64), BF16)
        self.mutril_h = din("mutril", (128, 64), BF16)
        self.imaskr_h = din("imaskr", (128, 64), BF16)
        self.recvm_h = din("recvm", (128, 8), F32)
        self.out_h = nc.dram_tensor("out", [NT, D], F32, kind="ExternalOutput").ap()
        self.dbg_h = nc.dram_tensor("dbg", [128, 512], F32, kind="ExternalOutput").ap()
        self.rows_h = nc.dram_tensor("rows_i", [128, NT], F32, kind="Internal").ap()
        self.rrows_h = nc.dram_tensor("rrows_i", [32, NT], BF16, kind="Internal").ap()
        self.stash_h = nc.dram_tensor("stash_i", [6144, NT], BF16, kind="Internal").ap()
        self.rowsT_h = nc.dram_tensor("rowsT_i", [6 * 2048, 16], F32, kind="Internal").ap()
        self.ccin_h = nc.dram_tensor("ccin_i", [128, 512], BF16, kind="Internal").ap()
        self.ccout_h = nc.dram_tensor("ccout_i", [128 * NCORE, 512], BF16,
                                      kind="Internal").ap()

        self.xnT = self.sb("xnT", (128, 8 * NT), BF16)
        self.PB = Pool([self.sb(f"pb{i}", (128, NT), BF16) for i in range(21)], "PB")
        self.PF = Pool([self.sb(f"pf{i}", (128, NT), F32) for i in range(4)], "PF")
        self.PB.sched = self.S
        self.PF.sched = self.S

        s = self.sb
        self.idbf = s("idbf_s", (128, 128), BF16)
        self.idf = s("idf_s", (128, 128), F32)
        self.hsel = s("hsel_s", (128, 2), BF16)
        self.mstrict = s("mstrict_s", (128, 64), BF16)
        self.mustrict = s("mustrict_s", (128, 64), BF16)
        self.mutril = s("mutril_s", (128, 64), BF16)
        self.imaskr = s("imaskr_s", (128, 64), BF16)
        self.recvm = s("recvm_s", (128, 8), F32)
        self.normw = s("normw_s", (128, 1), F32)
        self.dtb = s("dtb_s", (16, 1), F32)
        self.alog = s("alog_s", (16, 1), F32)
        self.negea = s("negea", (16, 1), F32)
        self.beta_r = s("beta_r", (16, NT), F32)
        self.g_r = s("g_r", (16, NT), F32)
        self.sc1_r = s("sc1_r", (16, NT), F32)
        self.sp_r = self.sc1_r
        self.com_r = self.sc1_r
        self.glr = s("glr", (16, NT), F32)
        self.com2_r = s("com2_r", (16, NT), F32)
        self.zer16 = s("zer16", (16, C), F32)
        self.l2s = self.glr
        self.fpair = s("fpair", (128, 6 * 8 * 32), F32)
        self.S32 = s("S32", (128, 128), F32)
        self.Sbf = s("Sbf", (128, 512), BF16)

        self.wtok = s("wtok", (128, 128), F32)
        self.wsc = s("wsc", (128, 128), F32)
        self.logit = self.beta_r
        self.besb = s("besb_s", (8, D), BF16)
        self.bgfsb = s("bgfsb_s", (8, 1), F32)
        self.wgfsb = s("wgfsb_s", (128, 64), BF16)
        self.wbsb = s("wbsb_s", (128, 128), BF16)
        self.wasb = s("wasb_s", (128, 128), BF16)
        self.rmsq = s("rmsq", (128, 64), F32)
        self.etb = s("etb", (128, 512), BF16)
        self.qtb = s("qtb", (128, 512), BF16)
        self.dbgsb = s("dbgsb", (128, 512), F32)

        self.cvals = s("cvals", (128, 8), F32)
        self.pbank = [self.ps(f"pk{i}", (128, 512), F32) for i in range(6)]
        self.pbt = [self.ps(f"pt{i}", (128, 1024), BF16) for i in range(2)]

        self.sems = {}
        for e in ENGS:
            self.sems[e + "_c"] = self.ctx.enter_context(self.nc.semaphore(e + "_c"))
        for e in ("sync", "gpsimd", "scalar"):
            self.sems[e + "_d"] = self.ctx.enter_context(self.nc.semaphore(e + "_d"))

    def fp(self, q, pair, j0=0, nj=NCH):
        c0 = (q * 32 + j0) * 8 + pair
        return self.fpair[:, c0:c0 + nj * 8:8]

    def load_wslice(self, w_hbm, sg, tag, waits=()):
        unit = self.PB.get(tag)
        ev = self.S.dma("gpsimd", lambda g, un=unit, wh=w_hbm, sg=sg:
                        g.dma_start(
                            out=un.rearrange("p (d c) -> p d c", d=8),
                            in_=wh[:, sg * 256:(sg + 1) * 256]
                            .rearrange("(d p) c -> p d c", p=128)),
                        waits=waits)
        return unit, ev

    def load_w1024(self, w_hbm, tag, waits=()):
        units = [self.PB.get(f"{tag}{i}") for i in range(4)]
        ev = None
        for u in range(4):
            ev = self.S.dma("gpsimd", lambda g, un=units[u], wh=w_hbm, u=u:
                            g.dma_start(
                                out=un.rearrange("p (a c) -> p a c", a=2),
                                in_=wh[u * 256:(u + 1) * 256, :]
                                .rearrange("(a p) c -> p a c", p=128)),
                            waits=waits)
        return units, ev

    @staticmethod
    def wsl(units, d, blk, width=128):
        return units[d // 2][:, (d % 2) * 1024 + blk * width:
                             (d % 2) * 1024 + (blk + 1) * width]

    def bank(self, i):
        return self.pbank[i], self.bank_free.get(i, [])

    def build(self):
        self.declare()
        self.p0_consts()
        self.p1_ln()
        self.p2_rows()
        if OPT["upto"] == "ln":
            return self.finish()
        self.ofin = []
        self.sgd_all = {}
        for sg in range(OPT.get("nsg", NSG)):
            self.attn_sg(sg)
        self.scan_phase()
        if OPT["upto"] == "attn":
            return self.finish()
        for half in (0, 1):
            if half == 0:
                self.wo_prefetch()
            self.wo_phase(half)
            if half == 0:
                self.tap("accA", self.ACC[0][:, 0:128])
            self.ffn_phase(half)
            if half == 0:
                self.tap("accB", self.ACC[0][:, 0:128])
            self.out_half(half)
        return self.finish()

    def p0_consts(self):
        S = self.S
        cev = None
        for dst, src in ((self.idbf, self.idbf_h), (self.idf, self.idf_h),
                         (self.hsel, self.hsel_h), (self.mstrict, self.mstrict_h),
                         (self.mustrict, self.mustrict_h), (self.mutril, self.mutril_h),
                         (self.imaskr, self.imaskr_h), (self.recvm, self.recvm_h),
                         (self.normw, self.normw_h), (self.dtb, self.dtb_h),
                         (self.alog, self.alog_h), (self.bgfsb, self.bgf_h)):
            cev = S.dma("sync", lambda sy, d=dst, s=src:
                        sy.dma_start(out=d[:, :], in_=s[:, :]))
        self.consts_s = cev
        gev = S.dma("gpsimd", lambda g: g.dma_start(out=self.besb[:, :],
                                                    in_=self.be_h[:, :]))
        for dst, src in ((self.wgfsb, self.wgf_h), (self.wbsb, self.wb_h),
                         (self.wasb, self.wa_h)):
            gev = S.dma("gpsimd", lambda g, d=dst, s=src: g.dma_start(
                out=d.rearrange("p (a e) -> p a e", a=8),
                in_=s.rearrange("(a p) e -> p a e", p=128)))
        self.consts_g = gev
        self.CONSTS = {1e-5: 0, 64e-6: 1, 1e-6: 2, -1.0: 3, 1.0 / 64: 4,
                       64.0: 5, 1.0: 6, 0.0: 7}
        for val, i in self.CONSTS.items():
            S.emit("gpsimd", lambda g, val=val, i=i:
                   g.memset(self.cvals[:, i:i + 1], float(val)), tick=True)
            self.nc.const_aps.register(F32, float(val), self.cvals[:, i:i + 1])
        S.fence(["gpsimd"])
        S.emit("gpsimd", lambda g: g.memset(self.zer16[:, :], 0.0), tick=True)
        S.emit("gpsimd", lambda g: g.memset(self.dbgsb[:, :], 0.0), tick=True)
        self.xcopy_ev = S.dma("sync", lambda sy:
                              sy.dma_start(out=self.out_h[:, :], in_=self.x_h[:, :]))
        ev = S.emit("scalar", lambda s: s.activation(self.negea[:, :],
                                                     self.alog[:, :], ACT.Exp),
                    waits=[self.consts_s], tick=True)
        S.emit("vector", lambda v: v.tensor_scalar_mul(self.negea[:, :],
                                                       self.negea[:, :], -1.0),
               waits=[ev], tick=True)

    def p1_ln(self):
        S, PF, PB = self.S, self.PF, self.PB
        xt = [PF.get(f"x{i}") for i in range(2)]
        sqt = PF.get("sq")
        stat = PF.get("stat")
        xn_tmp = [PB.get(f"xn{i}") for i in range(2)]
        e9d, tpmm, tpev, sqred = {}, {}, {}, {}
        for t in range(16):
            buf = xt[t % 2]
            ld = S.dma("sync", lambda sy, b=buf, t=t:
                       sy.dma_start(out=b[:, 0:1024],
                                    in_=self.x_h[t * 128:(t + 1) * 128, :]),
                       waits=[e9d.get(t - 2)])
            c = t * 8
            e1 = S.emit("vector", lambda v, b=buf, c=c:
                        v.reduce_sum(stat[:, c:c + 1], b[:, 0:1024], AX.X),
                        waits=[ld], tick=True)
            e2 = S.emit("scalar", lambda s, b=buf:
                        s.activation(sqt[:, 0:1024], b[:, 0:1024], ACT.Square),
                        waits=[ld, sqred.get(t - 1)], tick=True)
            e3 = S.emit("vector", lambda v, c=c:
                        v.reduce_sum(stat[:, c + 1:c + 2], sqt[:, 0:1024], AX.X),
                        waits=[e2], tick=True)
            sqred[t] = e3
            e4 = S.emit("vector", lambda v, c=c:
                        v.tensor_scalar_mul(stat[:, c + 2:c + 3], stat[:, c:c + 1],
                                            -1.0 / 1024), waits=[e1], tick=True)
            e5 = S.emit("vector", lambda v, c=c:
                        v.tensor_tensor(stat[:, c + 3:c + 4], stat[:, c + 2:c + 3],
                                        stat[:, c + 2:c + 3], ALU.mult),
                        waits=[e4], tick=True)
            e6 = S.emit("vector", lambda v, c=c:
                        v.scalar_tensor_tensor(stat[:, c + 4:c + 5],
                                               stat[:, c + 1:c + 2], 1.0 / 1024,
                                               stat[:, c + 3:c + 4],
                                               ALU.mult, ALU.subtract),
                        waits=[e3, e5], tick=True)
            e7 = S.emit("scalar", lambda s, c=c:
                        s.activation(stat[:, c + 5:c + 6], stat[:, c + 4:c + 5],
                                     ACT.Sqrt, bias=1e-5), waits=[e6], tick=True)
            e8 = S.emit("vector", lambda v, c=c:
                        v.reciprocal(stat[:, c + 6:c + 7], stat[:, c + 5:c + 6]),
                        waits=[e7], tick=True)
            xnb = xn_tmp[t % 2]
            e9 = S.emit("vector", lambda v, b=buf, xb=xnb, c=c:
                        v.tensor_scalar(xb[:, 0:1024], b[:, 0:1024],
                                        stat[:, c + 2:c + 3], stat[:, c + 6:c + 7],
                                        ALU.add, ALU.mult),
                        waits=[e8, tpmm.get(t - 2)], tick=True)
            e9d[t] = e9
            bank = self.pbt[t % 2]
            last = None
            for dd in range(8):
                last = S.emit("tensor", lambda tn, xb=xnb, dd=dd, bank=bank:
                              tn.transpose(bank[:, dd * 128:(dd + 1) * 128],
                                           xb[:, dd * 128:(dd + 1) * 128],
                                           self.idbf[:, :]),
                              waits=[e9, self.consts_s, tpev.get(t - 2)],
                              tick=(dd == 7))
            tpmm[t] = last
            tpev[t] = S.emit("scalar", lambda s, bank=bank, t=t:
                             s.copy(self.xnT.rearrange("p (d n) -> p d n", d=8)
                                    [:, :, t * 128:(t + 1) * 128],
                                    bank.rearrange("p (d n) -> p d n", d=8)),
                             waits=[last], tick=True)
        S.fence()
        self.tap("xt15", xt[15 % 2][:, 0:64])
        self.tap("stat", stat[:, 0:128])
        self.tap("xn15", xn_tmp[15 % 2][:, 0:64])
        for u in xt + [sqt, stat]:
            PF.put(u)
        for u in xn_tmp:
            PB.put(u)
        self.tap("xnT", self.xnT[:, 0:128])

    def proj_sg0(self):
        S, PB = self.S, self.PB
        pairs = (0, 1)
        qT, kT, gT, vT = {}, {}, {}, {}
        wus, wevs = {}, {}
        for qty in "qkgv":
            wus[qty], wevs[qty] = self.load_wslice(self.w_hs[qty], 0, f"w{qty}")
        for qty, store in (("q", qT), ("k", kT), ("g", gT), ("v", vT)):
            for p in pairs:
                store[p] = PB.get(f"{qty}T{p}")
            self.proj_pairs(wus[qty], wevs[qty], pairs, store)
        S.fence(["tensor", "vector", "scalar"])
        for qty in "qkgv":
            PB.put(wus[qty])
        for p in pairs:
            S.dma("sync", lambda sy, p=p: sy.dma_start(
                out=self.stash_h[p * 128:(p + 1) * 128, :], in_=gT[p][:, :]))
        S.fence(["sync"])
        for p in pairs:
            PB.put(gT[p])
        self._sg0 = (qT, kT, vT)

    def p2_rows(self):
        NT_ = ("sync", "vector", "scalar", "gpsimd")
        S = self.S
        for wi, (wsb, mode) in enumerate(((self.wbsb, "sig"), (self.wasb, "sp"))):
            for tc in range(4):
                bk, bev = self.bank((wi * 4 + tc) % 2)
                ev = None
                for d in range(8):
                    ev = S.emit("tensor", lambda tn, wsb=wsb, d=d, tc=tc, bk=bk:
                                tn.matmul(bk[0:16, 0:512],
                                          wsb[:, d * 16:(d + 1) * 16],
                                          self.xnT[:, d * NT + tc * 512:
                                                   d * NT + (tc + 1) * 512],
                                          start=(d == 0), stop=(d == 7)),
                                waits=([self.consts_g] + bev) if d == 0 else [],
                                tick=(d == 7))
                if mode == "sig":
                    e2 = S.emit("scalar", lambda s, bk=bk, tc=tc:
                                s.activation(self.beta_r[:, tc * 512:(tc + 1) * 512],
                                             bk[0:16, 0:512], ACT.Sigmoid),
                                waits=[ev], tick=True)
                else:
                    e2a = S.emit("scalar", lambda s, bk=bk, tc=tc:
                                 s.activation(self.sp_r[:, tc * 512:(tc + 1) * 512],
                                              bk[0:16, 0:512], ACT.Exp,
                                              bias=self.dtb[:, :]),
                                 waits=[ev], tick=True)
                    e2 = S.emit("scalar", lambda s, tc=tc:
                                s.activation(self.sp_r[:, tc * 512:(tc + 1) * 512],
                                             self.sp_r[:, tc * 512:(tc + 1) * 512],
                                             ACT.Ln, bias=1.0),
                                waits=[e2a], tick=True)
                self.bank_free[(wi * 4 + tc) % 2] = [e2]
        S.fence()
        S.emit("vector", lambda v: v.tensor_scalar(self.sc1_r[:, :], self.sc1_r[:, :],
                                                   self.negea[:, :], None, ALU.mult),
               tick=True)
        S.fence()
        for j in range(NCH):
            S.emit("vector", lambda v, j=j:
                   v.tensor_tensor_scan(self.g_r[:, j * C:(j + 1) * C],
                                        self.sp_r[:, j * C:(j + 1) * C],
                                        self.zer16[:, :], 0.0, ALU.add, ALU.add),
                   tick=(j == NCH - 1))
        S.fence()
        S.emit("vector", lambda v:
               v.tensor_copy(self.glr.rearrange("p (j o) -> p j o", o=C),
                             self.g_r.rearrange("p (j o) -> p j o", o=C)
                             [:, :, C - 1:C].broadcast_to((16, NCH, C))),
               tick=True)
        S.fence()
        S.emit("scalar", lambda s: s.activation(self.sc1_r[:, :], self.g_r[:, :],
                                                ACT.Exp), tick=True)
        S.fence(["scalar"])
        S.dma("sync", lambda sy: sy.dma_start(out=self.rows_h[5:128:8, :],
                                              in_=self.sc1_r[:, :]))
        S.emit("vector", lambda v: v.tensor_tensor(self.com2_r[:, :], self.beta_r[:, :],
                                                   self.sc1_r[:, :], ALU.mult),
               tick=True)
        S.fence(["vector"])
        S.dma("sync", lambda sy: sy.dma_start(out=self.rows_h[1:128:8, :],
                                              in_=self.com2_r[:, :]))
        S.emit("vector", lambda v: v.tensor_tensor(self.com2_r[:, :], self.glr[:, :],
                                                   self.g_r[:, :], ALU.subtract),
               waits=[("sync_d", S.dcnt["sync"])], tick=True)
        S.fence(["vector"])
        S.emit("scalar", lambda s: s.activation(self.com2_r[:, :], self.com2_r[:, :],
                                                ACT.Exp), tick=True)
        S.fence(["scalar"])
        S.dma("sync", lambda sy: sy.dma_start(out=self.rows_h[2:128:8, :],
                                              in_=self.com2_r[:, :]))
        S.emit("scalar", lambda s: s.activation(self.sc1_r[:, :], self.glr[:, :],
                                                ACT.Exp),
               waits=[("sync_d", S.dcnt["sync"])], tick=True)
        S.fence(["scalar"])
        S.dma("sync", lambda sy: sy.dma_start(out=self.rows_h[3:128:8, :],
                                              in_=self.sc1_r[:, :]))
        S.dma("sync", lambda sy: sy.dma_start(out=self.rows_h[0:128:8, :],
                                              in_=self.beta_r[:, :]))
        S.dma("sync", lambda sy: sy.dma_start(out=self.rows_h[4:128:8, :],
                                              in_=self.g_r[:, :]))
        rows_ev = S.fence()
        rowsrc = {0: self.beta_r, 4: self.g_r, 5: None, 1: None, 2: None, 3: None}
        stg = self.PF.get("rowstg")
        for q in range(6):
            rt = self.com2_r
            wprev = [("sync_d", S.dcnt["sync"])]
            lev = S.dma("sync", lambda sy, q=q, rt=rt: sy.dma_start(
                out=rt[:, :], in_=self.rows_h[q:128:8, :]), waits=wprev)
            bk = self.pbank[0]
            ev = None
            for m in range(16):
                ev = S.emit("tensor", lambda tn, m=m, bk=bk, rt=rt:
                            tn.transpose(bk[:, m * 16:(m + 1) * 16],
                                         rt[0:16, m * 128:(m + 1) * 128],
                                         self.idf[0:16, 0:16]),
                            waits=([lev] + self.bank_free.get(0, []))
                            if m == 0 else [],
                            tick=(m == 15))
            e2 = S.emit("vector", lambda v, bk=bk:
                        v.tensor_copy(stg[:, 0:256]
                                      .rearrange("p (m t r) -> p m t r", m=16, t=2),
                                      bk[:, 0:256].rearrange("p (m r t) -> p m t r",
                                                             m=16, t=2)),
                        waits=[ev], tick=True)
            self.bank_free[0] = [e2]
            S.dma("sync", lambda sy, q=q: sy.dma_start(
                out=self.rowsT_h[q * 2048:(q + 1) * 2048, :]
                .rearrange("(m p) c -> p m c", p=128),
                in_=stg[:, 0:256].rearrange("p (m c) -> p m c", m=16)),
                waits=[e2])
        S.fence(["sync"])
        for q in range(6):
            for half in range(2):
                S.dma("sync", lambda sy, q=q, half=half: sy.dma_start(
                    out=self.fpair[half * 64:(half + 1) * 64,
                                   q * 256:(q + 1) * 256]
                    .rearrange("p (j r) -> p j r", j=32),
                    in_=self.rowsT_h[q * 2048:(q + 1) * 2048,
                                     half * 8:(half + 1) * 8]
                    .rearrange("(j p) r -> p j r", p=64)))
        self.PF.put(stg)
        self.fpair_ev = S.fence()
        self.tap("g_r", self.g_r[:, 0:128])
        self.tap("fpair", self.fpair[:, 0:256])

    def proj_pairs(self, wu, wev, pairs, store, eng_cycle=("vector", "scalar")):
        S = self.S
        ei = 0
        pevs = {}
        for p in pairs:
            pl = p % 2
            for tc in range(4):
                bi = (p * 4 + tc) % 6
                bk = self.pbank[bi]
                ev = None
                for d in range(8):
                    w = [wev] + (self.bank_free.get(bi, []) if d == 0 else [])
                    ev = S.emit("tensor", lambda tn, d=d, pl=pl, tc=tc, bk=bk, wu=wu:
                                tn.matmul(bk[:, 0:512],
                                          wu[:, d * 256 + pl * 128:
                                             d * 256 + (pl + 1) * 128],
                                          self.xnT[:, d * NT + tc * 512:
                                                   d * NT + (tc + 1) * 512],
                                          start=(d == 0), stop=(d == 7)),
                                waits=w, tick=(d == 7))
                eng = eng_cycle[ei % len(eng_cycle)]
                ei += 1
                if eng == "scalar":
                    e2 = S.emit("scalar", lambda sc, bk=bk, p=p, tc=tc, st=store:
                                sc.copy(st[p][:, tc * 512:(tc + 1) * 512],
                                        bk[:, 0:512]),
                                waits=[ev], tick=True)
                else:
                    e2 = S.emit(eng, lambda v, bk=bk, p=p, tc=tc, st=store:
                                v.tensor_copy(st[p][:, tc * 512:(tc + 1) * 512],
                                              bk[:, 0:512]),
                                waits=[ev], tick=True)
                self.bank_free[bi] = [e2]
                pevs.setdefault(p, []).append(e2)
        return pevs

    def l2norm_rows(self, tile, p, qidx, scale8, ready=()):
        S, PB = self.S, self.PB
        q2 = self.q2buf
        prev = getattr(self, "_l2prev", None) or []
        esq = S.emit("scalar", lambda s: s.activation(q2[:, :], tile[:, :], ACT.Square),
                     waits=list(prev) + list(ready), tick=True)
        evs = []
        for tc in range(4):
            bi = 4 + tc % 2
            bk = self.pbank[bi]
            ev = S.emit("tensor", lambda tn, tc=tc, bk=bk:
                        tn.matmul(bk[0:2, 0:512], self.hsel[:, :],
                                  q2[:, tc * 512:(tc + 1) * 512],
                                  start=True, stop=True),
                        waits=[esq] + self.bank_free.get(bi, []), tick=True)
            e2 = S.emit("vector", lambda v, bk=bk, tc=tc:
                        v.tensor_copy(self.l2s[0:2, tc * 512:(tc + 1) * 512],
                                      bk[0:2, 0:512]),
                        waits=[ev], tick=True)
            self.bank_free[bi] = [e2]
            evs.append(e2)
        sc, bi_ = (64.0, 64e-6) if scale8 else (1.0, 1e-6)
        e3 = S.emit("scalar", lambda s: s.activation(self.l2s[0:2, :], self.l2s[0:2, :],
                                                     ACT.Ln, bias=bi_, scale=sc),
                    waits=evs, tick=True)
        e4 = S.emit("scalar", lambda s: s.activation(self.l2s[0:2, :], self.l2s[0:2, :],
                                                     ACT.Exp, scale=-0.5),
                    waits=[e3], tick=True)
        ev = S.dma("gpsimd", lambda g, p=p, qidx=qidx:
                   g.dma_start(out=self.rrows_h[4 * p + qidx:4 * p + qidx + 3:2, :],
                               in_=self.l2s[0:2, :]), waits=[e4])
        self._l2prev = evs + [ev]
        return ev

    def rrep(self, unit, p, qidx, bf_src=True, qslot=None, waits=()):
        S = self.S
        ev = None
        for hh in range(2):
            if bf_src:
                r0 = (2 * p + hh) * 2 + qidx
                src = self.rrows_h[r0:r0 + 1, :]
                eng = "sync"
            else:
                r0 = (2 * p + hh) * 8 + qslot
                src = self.rows_h[r0:r0 + 1, :]
                eng = "gpsimd"
            ev = S.dma(eng, lambda e, hh=hh, src=src, unit=unit:
                       e.dma_start(out=unit[hh * 64:(hh + 1) * 64, :],
                                   in_=src.broadcast_to((64, NT))),
                       waits=waits)
        return ev

    def attn_sg(self, sg):
        S, PB, PF = self.S, self.PB, self.PF
        pairs = (2 * sg, 2 * sg + 1)
        fence0 = S.fence()
        sg0pre = False
        self.GMs = {}
        self.GM_ev = {}
        for p in pairs:
            GM = PF.get(f"GM{p}")
            ev = None
            for hh in range(2):
                r0 = (2 * p + hh) * 8 + 4
                ev = S.dma("scalar", lambda sy, hh=hh, GM=GM, r0=r0: sy.dma_start(
                    out=GM[hh * 64:(hh + 1) * 64, :],
                    in_=self.rows_h[r0:r0 + 1, :].broadcast_to((64, NT))))
            self.GMs[p] = GM
            self.GM_ev[p] = ev

        if sg0pre:
            qT, kT, vT = self._sg0
        else:
            qT, kT, gT, vT = {}, {}, {}, {}
            wus = {}
            wevs = {}
            prjev = {}
            for qty in "qkgv":
                wus[qty], wevs[qty] = self.load_wslice(self.w_hs[qty], sg, f"w{qty}")
            self.q2buf = PB.get("q2")
            for p in pairs:
                for qty, store in (("q", qT), ("k", kT), ("g", gT), ("v", vT)):
                    store[p] = PB.get(f"{qty}T{p}")
                    prjev[(p, qty)] = self.proj_pairs(
                        wus[qty], wevs[qty], (p,), store)[p]
                self.l2norm_rows(qT[p], p, 1, scale8=True, ready=prjev[(p, "q")])
                self.l2norm_rows(kT[p], p, 0, scale8=False, ready=prjev[(p, "k")])
                S.dma("sync", lambda sy, p=p: sy.dma_start(
                    out=self.stash_h[p * 128:(p + 1) * 128, :], in_=gT[p][:, :]),
                    waits=prjev[(p, "g")])
            S.fence()
            PB.put(self.q2buf)
            for qty in "qkgv":
                PB.put(wus[qty])
            for p in pairs:
                PB.put(gT[p])
        self.vT_sg = vT
        self._did_norms = not sg0pre

        if not self._did_norms:
            self.q2buf = PB.get("q2")
            for p in pairs:
                self.l2norm_rows(qT[p], p, 1, scale8=True)
                self.l2norm_rows(kT[p], p, 0, scale8=False)
            S.fence()
            PB.put(self.q2buf)
        rb = [PB.get("rb0"), PB.get("rb1")]
        rix = 0
        last_mult = {}
        for p in pairs:
            for qidx, tile in ((1, qT[p]), (0, kT[p])):
                u = rb[rix % 2]
                rix += 1
                guard = [last_mult[rix - 2]] if rix - 2 in last_mult else []
                ev = self.rrep(u, p, qidx, bf_src=True, waits=guard)
                w = [ev]
                self.tap(f"R{qidx}_{p}", u[:, 0:128])
                last_mult[rix] = S.emit(
                    "vector", lambda v, t=tile, u=u:
                    v.tensor_tensor(t[:, :], t[:, :], u[:, :], ALU.mult),
                    waits=w, tick=True)
        S.fence()
        PB.put(rb[0]); PB.put(rb[1])
        self.tap(f"qTn{sg}", qT[pairs[0]][:, 0:256])
        self.tap(f"kTn{sg}", kT[pairs[0]][:, 0:256])

        sgd = {}
        st = {}
        p0, p1 = pairs
        Ls, LTs, LTt = self.build_L(p0)
        U0, V0, a0 = self.p_attn(p0, kT[p0], qT[p0], Ls, LTs, LTt)
        S.fence()
        for t in (Ls, LTs, LTt):
            PB.put(t)
        Ya0, Yb0, kt0 = self.trans_y(p0, kT[p0])
        S.fence()
        PB.put(kT[p0]); PB.put(self.vT_sg[p0])
        Ls, LTs, LTt = self.build_L(p1)
        self.iteration(p0, U0, V0, Ya0, Yb0)
        U1, V1, a1 = self.p_attn(p1, kT[p1], qT[p1], Ls, LTs, LTt)
        S.fence()
        for t in (Ls, LTs, LTt):
            PB.put(t)
        Ya1, Yb1, kt1 = self.trans_y(p1, kT[p1])
        S.fence()
        PB.put(kT[p1]); PB.put(self.vT_sg[p1])
        sgd[p0] = self.prework(p0, U0, V0, Ya0, Yb0, a0, kt0, qT[p0])
        for t in (U0, V0, Ya0, Yb0, a0, kt0, qT[p0]):
            PB.put(t)
        self.iteration(p1, U1, V1, Ya1, Yb1)
        S.fence()
        sgd[p1] = self.prework(p1, U1, V1, Ya1, Yb1, a1, kt1, qT[p1])
        for t in (U1, V1, Ya1, Yb1, a1, kt1, qT[p1]):
            PB.put(t)
        S.fence()

        for p in pairs:
            QsT, M3T, Cst, OVst = sgd[p]
            for si, t in ((16, QsT), (24, M3T), (32, Cst), (40, OVst)):
                eng = "sync" if si % 16 == 0 else "scalar"
                S.dma(eng, lambda sy, si=si, p=p, t=t: sy.dma_start(
                    out=self.stash_h[(si + p) * 128:(si + p + 1) * 128, :],
                    in_=t[:, :]))
        S.fence(["sync", "scalar"])
        hold = (sg == OPT.get("nsg", NSG) - 1 and OPT.get("nsg", NSG) == NSG)
        for p in pairs:
            QsT, M3T, Cst, OVst = sgd[p]
            if hold:
                self._m3cs_held[p] = (M3T, Cst)
                PB.put(QsT); PB.put(OVst)
            else:
                for t in sgd[p]:
                    PB.put(t)

    def scan_phase(self):
        S, PB = self.S, self.PB
        npair = 2 * OPT.get("nsg", NSG)
        M3 = {}
        Cs = {}
        lev = []
        held = getattr(self, "_m3cs_held", {})
        for p in range(npair):
            if p in held:
                M3[p], Cs[p] = held[p]
                continue
            M3[p] = PB.get(f"sM3{p}")
            Cs[p] = PB.get(f"sCs{p}")
            lev.append(S.dma("sync", lambda sy, p=p, t=M3[p]: sy.dma_start(
                out=t[:, :], in_=self.stash_h[(24 + p) * 128:(25 + p) * 128, :])))
            lev.append(S.dma("scalar", lambda sy, p=p, t=Cs[p]: sy.dma_start(
                out=t[:, :], in_=self.stash_h[(32 + p) * 128:(33 + p) * 128, :])))
        self._m3cs_held = {}
        ev0 = S.emit("gpsimd", lambda g: g.memset(self.Sbf[:, :], 0.0), tick=True)
        self.scan_multi(list(range(npair)), M3, Cs, None, None, None,
                        sb_ev=[ev0] + lev[-2:])
        for p in range(npair):
            PB.put(M3[p]); PB.put(Cs[p])
        self.tap("Spass1", self.Sbf[:, 0:256])
        xev = self.exchange_start()
        Qs = {}
        OV = {}
        pre_lev = []
        if npair == 8:
            for p in range(4):
                M3[p] = PB.get(f"sM3{p}")
                Cs[p] = PB.get(f"sCs{p}")
                Qs[p] = PB.get(f"sQs{p}")
                OV[p] = PB.get(f"sOV{p}")
                for si, t in ((24, M3[p]), (32, Cs[p]), (16, Qs[p]), (40, OV[p])):
                    eng = "sync" if si in (24, 16) else "scalar"
                    pre_lev.append(S.dma(eng, lambda sy, si=si, p=p, t=t:
                                         sy.dma_start(
                        out=t[:, :],
                        in_=self.stash_h[(si + p) * 128:(si + p + 1) * 128, :])))
        self.gating()
        self.exchange_finish(xev)
        self.tap("Sini", self.Sbf[:, 0:256])
        self.tap("SiniF", self.Sbf[:, 0:512])
        nb = max(1, npair // 4)
        for b in range(nb):
            bp = list(range(b * 4, b * 4 + 4))
            lev = []
            if b == 0 and pre_lev:
                lev = pre_lev
            else:
                for p in bp:
                    M3[p] = PB.get(f"sM3{p}")
                    Cs[p] = PB.get(f"sCs{p}")
                    Qs[p] = PB.get(f"sQs{p}")
                    OV[p] = PB.get(f"sOV{p}")
                    for si, t in ((24, M3[p]), (32, Cs[p]), (16, Qs[p]), (40, OV[p])):
                        eng = "sync" if si in (24, 16) else "scalar"
                        lev.append(S.dma(eng, lambda sy, si=si, p=p, t=t:
                                         sy.dma_start(
                            out=t[:, :],
                            in_=self.stash_h[(si + p) * 128:(si + p + 1) * 128, :])))
            ocm = {p: PB.get(f"ocm{p}") for p in bp}
            self.scan_multi(bp, M3, Cs, Qs, OV, ocm,
                            sb_ev=lev[-2:] + [self._xfin])
            S.fence()
            for p in bp:
                self.tap(f"ocmA{p}", ocm[p][:, 0:64])
            for p in bp:
                PB.put(M3[p]); PB.put(Cs[p]); PB.put(Qs[p]); PB.put(OV[p])
            self.gtbuf = {}
            self.gt_ev = {}
            for p in bp:
                gt = PB.get(f"gt{p}")
                self.gtbuf[p] = gt
                self.gt_ev[p] = S.dma("sync", lambda sy, p=p, gt=gt: sy.dma_start(
                    out=gt[:, :], in_=self.stash_h[p * 128:(p + 1) * 128, :]))
            self.sqbuf = PB.get("osq")
            self._sq_prev = None
            for p in bp:
                self.opost(p, ocm[p])
                PB.put(ocm[p])
            PB.put(self.sqbuf)
            S.fence()

    def scan_multi(self, plist, M3, Cs, Qs, OV, ocm, sb_ev):
        S = self.S
        pass2 = Qs is not None
        evS = list(sb_ev)
        oces = []
        for j in range(NCH):
            bis = 4 + j % 2
            bks = self.pbank[bis]
            w0 = evS + self.bank_free.get(bis, [])
            ev = None
            if pass2:
                bio = 2 + j % 2
                bko = self.pbank[bio]
                w0 = w0 + self.bank_free.get(bio, [])
            n = len(plist)
            for ii, p in enumerate(plist):
                for hh in range(2):
                    hr = slice(64 * hh, 64 * hh + 64)
                    tp = (64 * hh, 64 * hh)
                    last = (ii == n - 1 and hh == 1)
                    ev = S.emit("tensor", lambda tn, hr=hr, tp=tp, p=p, j=j,
                                M3T=M3[p], bks=bks:
                                tn.matmul(bks[hr, p * 64 % 512:(p * 64 % 512) + 64],
                                          M3T[hr, j * C:(j + 1) * C],
                                          self.Sbf[hr, p * 64:(p + 1) * 64],
                                          start=True, stop=True, tile_position=tp),
                                waits=w0 if (ii == 0 and hh == 0) else [],
                                tick=(not pass2 and last))
                    if pass2:
                        ev = S.emit("tensor", lambda tn, hr=hr, tp=tp, p=p, j=j,
                                    QsT=Qs[p], bko=bko:
                                    tn.matmul(bko[hr, (p % 4) * 64:(p % 4) * 64 + 64],
                                              QsT[hr, j * C:(j + 1) * C],
                                              self.Sbf[hr, p * 64:(p + 1) * 64],
                                              start=True, stop=True,
                                              tile_position=tp),
                                    tick=last)
            sevs = []
            for ii, p in enumerate(plist):
                e = S.emit("vector", lambda v, p=p, j=j, Cst=Cs[p], bks=bks:
                           v.tensor_tensor(self.Sbf[:, p * 64:(p + 1) * 64],
                                           bks[:, p * 64 % 512:(p * 64 % 512) + 64],
                                           Cst[:, j * C:(j + 1) * C], ALU.add),
                           waits=[ev], tick=True)
                sevs.append(e)
                if pass2:
                    e2 = S.emit("scalar", lambda sc, p=p, j=j, bko=bko:
                                sc.copy(ocm[p][:, j * C:(j + 1) * C],
                                        bko[:, (p % 4) * 64:(p % 4) * 64 + 64]),
                                waits=[ev], tick=True)
                    if j >= NCH - 2:
                        oces.append(e2)
                    self.bank_free[bio] = [e2]
            evS = sevs
            self.bank_free[bis] = sevs
        if pass2:
            for p in plist:
                S.emit("vector", lambda v, p=p, OVst=OV[p]:
                       v.tensor_tensor(ocm[p][:, :], ocm[p][:, :], OVst[:, :],
                                       ALU.add),
                       waits=oces, tick=True)

    def exchange_start(self):
        S = self.S
        sf = S.fence()
        ev = S.dma("sync", lambda sy: sy.dma_start(out=self.ccin_h[:, :],
                                                   in_=self.Sbf[:, :]))
        cev = S.emit("gpsimd", lambda g: g.collective_compute(
            "AllGather", ALU.bypass,
            replica_groups=[list(range(NCORE))],
            ins=[self.ccin_h[:, :]], outs=[self.ccout_h[:, :]]),
            waits=[ev], tick=True)
        return cev

    def exchange_finish(self, cev):
        S, PB = self.S, self.PB
        g0 = PB.get("gath0")
        g1 = PB.get("gath1")
        gev = []
        for gi, gu in ((0, g0), (1, g1)):
            gev.append(S.dma("sync", lambda sy, gi=gi, gu=gu: sy.dma_start(
                out=gu.rearrange("p (r c) -> p r c", r=4),
                in_=self.ccout_h.rearrange("(r p) c -> p r c", p=128)
                [:, gi * 4:(gi + 1) * 4, :]),
                waits=[cev]))
        e = S.emit("vector", lambda v: v.tensor_scalar(
            self.Sbf[:, :], g0[:, 0:512], self.recvm[:, 0:1], None, ALU.mult),
            waits=gev, tick=True)
        for r in range(1, 8):
            gu = g0 if r < 4 else g1
            e = S.emit("vector", lambda v, r=r, gu=gu: v.scalar_tensor_tensor(
                self.Sbf[:, :], gu[:, (r % 4) * 512:(r % 4 + 1) * 512],
                self.recvm[:, r:r + 1], self.Sbf[:, :], ALU.mult, ALU.add),
                waits=[e], tick=True)
        self._xfin = e
        PB.put(g0); PB.put(g1)

    def build_L(self, p):
        S, PB, PF = self.S, self.PB, self.PF
        GM = self.GMs[p]
        Dt = PF.get("Dt")
        ev = self.GM_ev[p]
        gD = S.emit("vector", lambda v: v.tensor_tensor(
            Dt.rearrange("p (j m) -> p j m", m=C),
            GM.rearrange("p (j m) -> p j m", m=C),
            bc(self.fp(4, p), NCH, C), ALU.subtract),
            waits=[ev, self.fpair_ev[-1] if self.fpair_ev else None], tick=True)
        e1 = S.emit("vector", lambda v: v.tensor_scalar_max(GM[:, :], Dt[:, :], 0.0),
                    waits=[gD], tick=True)
        e2 = S.emit("vector", lambda v: v.tensor_scalar_min(Dt[:, :], Dt[:, :], 0.0),
                    waits=[gD, e1], tick=True)
        Ls = PB.get(f"Ls{p}")
        LTt = PB.get(f"LTt{p}")
        LTs = PB.get(f"LTs{p}")
        e3 = S.emit("scalar", lambda s: s.activation(Ls[:, :], GM[:, :], ACT.Exp,
                                                     scale=-1.0),
                    waits=[e1], tick=True)
        e4 = S.emit("scalar", lambda s: s.activation(LTt[:, :], Dt[:, :], ACT.Exp),
                    waits=[e2], tick=True)
        m4 = lambda t: t.rearrange("p (o c) -> p o c", c=64)
        mb = lambda mk: mk.rearrange("p (o c) -> p o c", o=1).broadcast_to((128, 32, 64))
        e5 = S.emit("vector", lambda v: v.tensor_tensor(m4(LTs), m4(LTt),
                                                        mb(self.mustrict), ALU.mult),
                    waits=[e4], tick=True)
        e6 = S.emit("vector", lambda v: v.tensor_tensor(m4(LTt), m4(LTt),
                                                        mb(self.mutril), ALU.mult),
                    waits=[e5], tick=True)
        e7 = S.emit("vector", lambda v: v.tensor_tensor(m4(Ls), m4(Ls),
                                                        mb(self.mstrict), ALU.mult),
                    waits=[e3], tick=True)
        e8 = S.emit("vector", lambda v: v.tensor_tensor(
            Ls.rearrange("p (j m) -> p j m", m=C),
            Ls.rearrange("p (j m) -> p j m", m=C),
            bc(self.fp(0, p), NCH, C), ALU.mult), waits=[e7], tick=True)
        brep = PB.get("brep")
        ev = self.rrep(brep, p, 0, bf_src=False, qslot=0)
        e9 = S.emit("vector", lambda v: v.tensor_tensor(LTs[:, :], LTs[:, :],
                                                        brep[:, :], ALU.mult),
                    waits=[ev, e5], tick=True)
        self.L_evs = [e6, e8, e9]
        PF.put(GM); PF.put(Dt); PB.put(brep)
        return Ls, LTs, LTt

    def p_attn(self, p, kTn, qTn, Ls, LTs, LTt):
        S, PB = self.S, self.PB
        U = PB.get(f"U{p}")
        V = PB.get(f"V{p}")
        aT = PB.get(f"aT{p}")
        for jg in range(4):
            biP = 2 * (jg % 3)
            biA = biP + 1
            bkP, bkA = self.pbank[biP], self.pbank[biA]
            ev = None
            w0 = (self.bank_free.get(biP, []) + self.bank_free.get(biA, [])
                  + self.L_evs)
            for jj in range(8):
                j = jg * 8 + jj
                for hh in range(2):
                    hr = slice(64 * hh, 64 * hh + 64)
                    tp = (64 * hh, 64 * hh)
                    S.emit("tensor", lambda tn, hr=hr, tp=tp, j=j, jj=jj, bkP=bkP:
                           tn.matmul(bkP[hr, jj * 64:(jj + 1) * 64],
                                     kTn[hr, j * C:(j + 1) * C],
                                     kTn[hr, j * C:(j + 1) * C],
                                     start=True, stop=True, tile_position=tp),
                           waits=w0 if (jj == 0 and hh == 0) else [])
                    ev = S.emit("tensor", lambda tn, hr=hr, tp=tp, j=j, jj=jj, bkA=bkA:
                                tn.matmul(bkA[hr, jj * 64:(jj + 1) * 64],
                                          kTn[hr, j * C:(j + 1) * C],
                                          qTn[hr, j * C:(j + 1) * C],
                                          start=True, stop=True, tile_position=tp),
                                tick=(jj == 7 and hh == 1))
            sl = slice(jg * 512, (jg + 1) * 512)
            eV = S.emit("vector", lambda v, sl=sl, bkP=bkP:
                        v.tensor_tensor(V[:, sl], bkP[:, 0:512], Ls[:, sl], ALU.mult),
                        waits=[ev], tick=True)
            eU = S.emit("vector", lambda v, sl=sl, bkP=bkP:
                        v.tensor_tensor(U[:, sl], bkP[:, 0:512], LTs[:, sl], ALU.mult),
                        waits=[ev], tick=True)
            eA = S.emit("vector", lambda v, sl=sl, bkA=bkA:
                        v.tensor_tensor(aT[:, sl], bkA[:, 0:512], LTt[:, sl], ALU.mult),
                        waits=[ev], tick=True)
            self.bank_free[biP] = [eV, eU]
            self.bank_free[biA] = [eA]
        return U, V, aT

    def trans_y(self, p, kTn):
        S, PB = self.S, self.PB
        vTp = self.vT_sg[p]
        Ya = PB.get(f"Y{p}a")
        Yb = PB.get(f"Y{p}b")
        kt = PB.get(f"kt{p}")
        for jg in range(4):
            Yu = Ya if jg < 2 else Yb
            yv = Yu.rearrange("p (r b) -> p r b", b=128)
            for src, bkt_i, dsts in ((kTn, 0, "ku"), (vTp, 1, "v")):
                bkt = self.pbt[bkt_i]
                ev = None
                w0 = self.bt_free.get(bkt_i, [])
                for jj in range(8):
                    j = jg * 8 + jj
                    for hh in range(2):
                        hr = slice(64 * hh, 64 * hh + 64)
                        ev = S.emit("tensor",
                                    lambda tn, hr=hr, hh=hh, j=j, jj=jj, bkt=bkt, src=src:
                                    tn.transpose(bkt[hr, jj * 64:(jj + 1) * 64],
                                                 src[hr, j * C:(j + 1) * C],
                                                 self.idbf[hr, hh * 64:hh * 64 + 64],
                                                 tile_position=(64 * hh, 64 * hh)),
                                    waits=w0 if (jj == 0 and hh == 0) else [],
                                    tick=(jj == 7 and hh == 1))
                bv = bkt.rearrange("p (r b) -> p r b", b=64)[:, 0:8, :]
                r0 = (jg % 2) * 8
                if dsts == "ku":
                    e1 = S.emit("vector", lambda v, yv=yv, r0=r0, bv=bv, jg=jg:
                                v.tensor_tensor(yv[:, r0:r0 + 8, 64:128], bv,
                                                bc(self.fp(1, p, jg * 8, 8), 8, C),
                                                ALU.mult),
                                waits=[ev], tick=True)
                    e2 = S.emit("vector", lambda v, r0=r0, bv=bv, jg=jg:
                                v.tensor_tensor(kt.rearrange("p (r b) -> p r b", b=64)
                                                [:, jg * 8:jg * 8 + 8, :], bv,
                                                bc(self.fp(2, p, jg * 8, 8), 8, C),
                                                ALU.mult),
                                waits=[ev], tick=True)
                    self.bt_free[bkt_i] = [e1, e2]
                else:
                    e1 = S.emit("vector", lambda v, yv=yv, r0=r0, bv=bv, jg=jg:
                                v.tensor_tensor(yv[:, r0:r0 + 8, 0:64], bv,
                                                bc(self.fp(0, p, jg * 8, 8), 8, C),
                                                ALU.mult),
                                waits=[ev], tick=True)
                    self.bt_free[bkt_i] = [e1]
        return Ya, Yb, kt

    def iteration(self, p, U, V, Ya, Yb):
        S = self.S
        evY, evU, evV = {}, {}, {}
        for s in range(OPT.get('iter_rounds', 6)):
            for jg in range(8):
                Yu = Ya if jg < 4 else Yb
                bia = 2 * (jg % 3)
                bis = bia + 1
                bka, bks = self.pbank[bia], self.pbank[bis]
                w0 = (self.bank_free.get(bia, []) + self.bank_free.get(bis, [])
                      + [evY.get(jg), evU.get(jg), evV.get(jg)])
                ev = None
                for jj in range(4):
                    j = jg * 4 + jj
                    jr = j % 16
                    for hh in range(2):
                        hr = slice(64 * hh, 64 * hh + 64)
                        tp = (64 * hh, 64 * hh)
                        S.emit("tensor", lambda tn, hr=hr, tp=tp, j=j, jj=jj,
                               bka=bka, Yu=Yu, jr=jr:
                               tn.matmul(bka[hr, jj * 128:(jj + 1) * 128],
                                         U[hr, j * C:(j + 1) * C],
                                         Yu[hr, jr * 128:(jr + 1) * 128],
                                         start=True, stop=True, tile_position=tp),
                               waits=w0 if (jj == 0 and hh == 0) else [])
                        if s < 5:
                            S.emit("tensor", lambda tn, hr=hr, tp=tp, j=j, jj=jj, bks=bks:
                                   tn.matmul(bks[hr, jj * 64:(jj + 1) * 64],
                                             V[hr, j * C:(j + 1) * C],
                                             U[hr, j * C:(j + 1) * C],
                                             start=True, stop=True, tile_position=tp))
                        if s < 4:
                            S.emit("tensor", lambda tn, hr=hr, tp=tp, j=j, jj=jj, bks=bks:
                                   tn.matmul(bks[hr, 256 + jj * 64:256 + (jj + 1) * 64],
                                             U[hr, j * C:(j + 1) * C],
                                             V[hr, j * C:(j + 1) * C],
                                             start=True, stop=True, tile_position=tp))
                ev = S.tick_last("tensor")
                yl = Yu.rearrange("p (r b) -> p r b", b=128)
                r0 = (jg % 4) * 4
                bkav = bka.rearrange("p (r b) -> p r b", b=128)
                evY[jg] = S.emit("vector", lambda v, yl=yl, r0=r0, bkav=bkav:
                                 v.tensor_tensor(yl[:, r0:r0 + 4, :],
                                                 bkav, yl[:, r0:r0 + 4, :], ALU.add),
                                 waits=[ev], tick=True)
                frees = [evY[jg]]
                if s < 5:
                    evU[jg] = S.emit("scalar", lambda sc, jg=jg, bks=bks:
                                     sc.copy(U[:, jg * 256:(jg + 1) * 256],
                                             bks[:, 0:256]),
                                     waits=[ev], tick=True)
                    frees.append(evU[jg])
                if s < 4:
                    evV[jg] = S.emit("scalar", lambda sc, jg=jg, bks=bks:
                                     sc.copy(V[:, jg * 256:(jg + 1) * 256],
                                             bks[:, 256:512]),
                                     waits=[ev], tick=True)
                    frees.append(evV[jg])
                self.bank_free[bia] = [evY[jg]]
                self.bank_free[bis] = frees[1:] or [evY[jg]]

    def prework(self, p, U, V, Ya, Yb, aT, kt, qTn):
        S, PB = self.S, self.PB
        QsT = PB.get(f"Qs{p}")
        M3T = PB.get(f"M3{p}")
        Cst = PB.get(f"Cs{p}")
        OVst = PB.get(f"OV{p}")
        EG = PB.get("EGrep")
        egv = self.rrep(EG, p, 5, bf_src=False, qslot=5)
        et = self.etb
        et2 = self.qtb
        e_qprev = getattr(self, "_pw_eq", None)
        e_iprev = getattr(self, "_pw_ei", None)
        for jg in range(4):
            sl4 = slice(jg * 512, (jg + 1) * 512)
            e_q = S.emit("vector", lambda v, sl4=sl4: v.tensor_tensor(
                et2[:, :], qTn[:, sl4], EG[:, sl4], ALU.mult),
                waits=[egv, e_qprev], tick=True)
            e_i = S.emit("vector", lambda v, jg=jg: v.tensor_tensor(
                et.rearrange("p (r b) -> p r b", b=64),
                self.imaskr.rearrange("p (o b) -> p o b", o=1)
                .broadcast_to((128, 8, 64)),
                bc(self.fp(3, p, jg * 8, 8), 8, C), ALU.mult),
                waits=[e_iprev], tick=True)
            bks = [self.pbank[i] for i in range(4)]
            ev = None
            w0 = sum((self.bank_free.get(i, []) for i in range(4)), [])
            for jj in range(8):
                j = jg * 8 + jj
                jr = j % 16
                Yu = Ya if j < 16 else Yb
                for hh in range(2):
                    hr = slice(64 * hh, 64 * hh + 64)
                    tp = (64 * hh, 64 * hh)
                    kcd = Yu[hr, jr * 128 + 64:jr * 128 + 128]
                    vp = Yu[hr, jr * 128:jr * 128 + 64]
                    js = slice(jj * 64, (jj + 1) * 64)
                    S.emit("tensor", lambda tn, hr=hr, tp=tp, j=j, js=js,
                           kcd=kcd, b=bks[0]:
                           tn.matmul(b[hr, js], kcd, aT[hr, j * C:(j + 1) * C],
                                     start=True, stop=True, tile_position=tp),
                           waits=w0 if (jj == 0 and hh == 0) else [])
                    S.emit("tensor", lambda tn, hr=hr, tp=tp, j=j, js=js,
                           kcd=kcd, b=bks[1]:
                           tn.matmul(b[hr, js], kcd, kt[hr, j * C:(j + 1) * C],
                                     start=True, stop=True, tile_position=tp))
                    S.emit("tensor", lambda tn, hr=hr, tp=tp, j=j, js=js,
                           vp=vp, b=bks[2]:
                           tn.matmul(b[hr, js], kt[hr, j * C:(j + 1) * C], vp,
                                     start=True, stop=True, tile_position=tp))
                    S.emit("tensor", lambda tn, hr=hr, tp=tp, j=j, js=js,
                           vp=vp, b=bks[3]:
                           tn.matmul(b[hr, js], aT[hr, j * C:(j + 1) * C], vp,
                                     start=True, stop=True, tile_position=tp))
            ev = S.tick_last("tensor")
            sl = slice(jg * 512, (jg + 1) * 512)
            e0 = S.emit("vector", lambda v, sl=sl, b=bks[0]:
                        v.scalar_tensor_tensor(QsT[:, sl], b[:, 0:512], -1.0,
                                               et2[:, :], ALU.mult, ALU.add),
                        waits=[ev, e_q], tick=True)
            e1 = S.emit("vector", lambda v, sl=sl, b=bks[1]:
                        v.scalar_tensor_tensor(M3T[:, sl], b[:, 0:512], -1.0,
                                               et[:, :], ALU.mult, ALU.add),
                        waits=[ev, e_i], tick=True)
            e_qprev, e_iprev = e0, e1
            e2 = S.emit("scalar", lambda s, sl=sl, b=bks[2]:
                        s.copy(Cst[:, sl], b[:, 0:512]), waits=[ev], tick=True)
            e3 = S.emit("scalar", lambda s, sl=sl, b=bks[3]:
                        s.copy(OVst[:, sl], b[:, 0:512]), waits=[ev], tick=True)
            for i, e in enumerate((e0, e1, e2, e3)):
                self.bank_free[i] = [e]
        self._pw_eq, self._pw_ei = e_qprev, e_iprev
        S.fence()
        PB.put(EG)
        return QsT, M3T, Cst, OVst

    def opost(self, p, oc):
        S, PB = self.S, self.PB
        sq = self.sqbuf
        e1 = S.emit("scalar", lambda s: s.activation(sq[:, :], oc[:, :], ACT.Square),
                    waits=[getattr(self, "_sq_prev", None)], tick=True)
        e2 = S.emit("vector", lambda v: v.tensor_reduce(
            self.rmsq[:, 0:32], sq.rearrange("p (j m) -> p j m", m=C), AX.X, ALU.add),
            waits=[e1, getattr(self, "_opost_prev", None),
                   getattr(self, "_gate_done", None)], tick=True)
        self._sq_prev = e2
        e3 = S.emit("scalar", lambda s: s.activation(self.rmsq[:, 32:64],
                                                     self.rmsq[:, 0:32], ACT.Ln,
                                                     scale=1.0 / 64, bias=1e-5),
                    waits=[e2], tick=True)
        e4 = S.emit("scalar", lambda s: s.activation(self.rmsq[:, 0:32],
                                                     self.rmsq[:, 32:64], ACT.Exp,
                                                     scale=-0.5),
                    waits=[e3], tick=True)
        e5 = S.emit("vector", lambda v: v.tensor_tensor(
            oc.rearrange("p (j m) -> p j m", m=C),
            oc.rearrange("p (j m) -> p j m", m=C),
            bc(self.rmsq[:, 0:32], 32, C), ALU.mult), waits=[e4], tick=True)
        self._opost_prev = e5
        self.tap(f"orms{p}", oc[:, 0:128])
        gt = self.gtbuf[p]
        e6 = S.emit("scalar", lambda s: s.activation(gt[:, :], gt[:, :], ACT.Silu),
                    waits=[self.gt_ev[p]], tick=True)
        ofin = PB.get(f"ofin{p}")
        e7s = []
        for jg in range(4):
            bkt = self.pbt[jg % 2]
            ev = None
            w0 = self.bt_free.get(jg % 2, []) + [e5]
            for jj in range(8):
                j = jg * 8 + jj
                for hh in range(2):
                    hr = slice(64 * hh, 64 * hh + 64)
                    ev = S.emit("tensor", lambda tn, hr=hr, hh=hh, j=j, jj=jj, bkt=bkt:
                                tn.transpose(bkt[hr, jj * 64:(jj + 1) * 64],
                                             oc[hr, j * C:(j + 1) * C],
                                             self.idbf[hr, hh * 64:hh * 64 + 64],
                                             tile_position=(64 * hh, 64 * hh)),
                                waits=w0 if (jj == 0 and hh == 0) else [],
                                tick=(jj == 7 and hh == 1))
            e7 = S.emit("vector", lambda v, jg=jg, bkt=bkt:
                        v.tensor_scalar(ofin[:, jg * 512:(jg + 1) * 512],
                                        bkt[:, 0:512], self.normw[:, :], None,
                                        ALU.mult),
                        waits=[ev], tick=True)
            e7s.append(e7)
            self.bt_free[jg % 2] = [e7]
        self.tap(f"ofpre{p}", ofin[:, 0:128])
        self.tap(f"gt{p}", gt[:, 0:128])
        e8 = S.emit("vector", lambda v: v.tensor_tensor(ofin[:, :], ofin[:, :],
                                                        gt[:, :], ALU.mult),
                    waits=[e6] + e7s, tick=True)
        sev = S.dma("sync", lambda sy: sy.dma_start(out=self.stash_h[(8 + p) * 128:(9 + p) * 128, :],
                                                    in_=ofin[:, :]), waits=[e8])
        S.fence()
        PB.put(gt)
        if p >= 4 and OPT.get("nsg", NSG) == NSG and OPT["upto"] == "all":
            self._ofin_held[p] = ofin
        else:
            PB.put(ofin)
        self.tap(f"ofin{p}", ofin[:, 0:256])

    def wo_prefetch(self):
        S, PB = self.S, self.PB
        of = {}
        held = self._ofin_held
        self._ofin_held = {}
        for p in range(8):
            if p in held:
                of[p] = held[p]
                continue
            of[p] = PB.get(f"wof{p}")
            S.dma("sync", lambda sy, p=p, of=of: sy.dma_start(
                out=of[p][:, :],
                in_=self.stash_h[(8 + p) * 128:(9 + p) * 128, :]))
        wu, wev = self.load_w1024(self.w_hs["o"], "wo")
        self._wo_pre = (of, wu, wev)

    def wo_phase(self, half):
        S, PB, PF = self.S, self.PB, self.PF
        if half == 1:
            self.wo_prefetch()
        of, wu, wev = self._wo_pre
        S.fence(["sync", "gpsimd"])
        if half == 0:
            self.ACC = [PF.get(f"acc{i}") for i in range(4)]
        ei = 0
        for tl in range(8):
            t = half * 8 + tl
            for och in range(2):
                bi = (2 * tl + och) % 6
                bk = self.pbank[bi]
                ev = None
                for p in range(8):
                    w = (self.bank_free.get(bi, []) + [wev]) if p == 0 else []
                    ev = S.emit("tensor", lambda tn, p=p, t=t, och=och, bk=bk, wu=wu:
                                tn.matmul(bk[:, 0:512],
                                          of[p][:, t * 128:(t + 1) * 128],
                                          self.wsl(wu, p, och, 512),
                                          start=(p == 0), stop=(p == 7)),
                                waits=w, tick=(p == 7))
                au = self.ACC[tl // 2]
                asl = slice((tl % 2) * 1024 + och * 512,
                            (tl % 2) * 1024 + (och + 1) * 512)
                if ei % 2 == 0:
                    e2 = S.emit("vector", lambda v, au=au, asl=asl, bk=bk:
                                v.tensor_copy(au[:, asl], bk[:, 0:512]),
                                waits=[ev], tick=True)
                else:
                    e2 = S.emit("scalar", lambda sc, au=au, asl=asl, bk=bk:
                                sc.copy(au[:, asl], bk[:, 0:512]),
                                waits=[ev], tick=True)
                ei += 1
                self.bank_free[bi] = [e2]
        S.fence()
        for p in range(8):
            PB.put(of[p])
        for u in wu:
            PB.put(u)

    def gating(self):
        S, PB = self.S, self.PB
        for tc in range(4):
            bi = 4 + tc % 2
            bk = self.pbank[bi]
            ev = None
            for d in range(8):
                ev = S.emit("tensor", lambda tn, d=d, tc=tc, bk=bk:
                            tn.matmul(bk[0:8, 0:512],
                                      self.wgfsb[:, d * 8:(d + 1) * 8],
                                      self.xnT[:, d * NT + tc * 512:
                                               d * NT + (tc + 1) * 512],
                                      start=(d == 0), stop=(d == 7)),
                            waits=self.bank_free.get(bi, []) if d == 0 else [],
                            tick=(d == 7))
            e2 = S.emit("vector", lambda v, bk=bk, tc=tc:
                        v.tensor_scalar(self.logit[0:8, tc * 512:(tc + 1) * 512],
                                        bk[0:8, 0:512], self.bgfsb[:, :], None,
                                        ALU.add),
                        waits=[ev], tick=True)
            self.bank_free[bi] = [e2]
        S.fence()
        bk = self.pbank[0]
        ev = None
        for t in range(16):
            ev = S.emit("tensor", lambda tn, t=t, bk=bk:
                        tn.transpose(bk[:, t * 8:(t + 1) * 8],
                                     self.logit[0:8, t * 128:(t + 1) * 128],
                                     self.idf[0:8, 0:8]),
                        waits=self.bank_free.get(0, []) if t == 0 else [],
                        tick=(t == 15))
        e = S.emit("vector", lambda v: v.tensor_copy(self.wtok[:, :], bk[:, 0:128]),
                   waits=[ev], tick=True)
        self.bank_free[0] = [e]
        wv = self.wtok.rearrange("p (t e) -> p t e", t=16)
        sc = self.wsc
        av = sc.rearrange("p (t e) -> p t e", t=16)
        m1 = self.rmsq[:, 0:16]
        m2 = self.rmsq[:, 16:32]
        den = self.rmsq[:, 32:48]
        e1 = S.emit("vector", lambda v: v.reduce_max(m1, wv, AX.X), waits=[e], tick=True)
        e2 = S.emit("vector", lambda v: v.tensor_tensor(av, wv, bc(m1, 16, 8),
                                                        ALU.is_ge), waits=[e1], tick=True)
        e3 = S.emit("vector", lambda v: v.scalar_tensor_tensor(
            sc[:, :], sc[:, :], -1e30, self.wtok[:, :], ALU.mult, ALU.add),
            waits=[e2], tick=True)
        e4 = S.emit("vector", lambda v: v.reduce_max(m2, av, AX.X), waits=[e3], tick=True)
        e5 = S.emit("vector", lambda v: v.tensor_tensor(av, wv, bc(m2, 16, 8),
                                                        ALU.is_ge), waits=[e4], tick=True)
        e6 = S.emit("vector", lambda v: v.tensor_tensor(den, m2, m1, ALU.subtract),
                    waits=[e4], tick=True)
        e7 = S.emit("scalar", lambda s: s.activation(den, den, ACT.Exp),
                    waits=[e6], tick=True)
        e8 = S.emit("vector", lambda v: v.tensor_scalar_add(den, den, 1.0),
                    waits=[e7], tick=True)
        e9 = S.emit("vector", lambda v: v.reciprocal(den, den), waits=[e8], tick=True)
        e10 = S.emit("vector", lambda v: v.tensor_tensor(wv, wv, bc(m1, 16, 8),
                                                         ALU.subtract),
                     waits=[e5], tick=True)
        e11 = S.emit("scalar", lambda s: s.activation(self.wtok[:, :], self.wtok[:, :],
                                                      ACT.Exp), waits=[e10], tick=True)
        e12 = S.emit("vector", lambda v: v.tensor_tensor(wv, wv, av, ALU.mult),
                     waits=[e11], tick=True)
        e13 = S.emit("vector", lambda v: v.tensor_tensor(wv, wv, bc(den, 16, 8),
                                                         ALU.mult),
                     waits=[e9, e12], tick=True)
        self._gate_done = e13
        self.tap("wtok", self.wtok[:, :])
        self.wTb = PB.get("wTb")
        wTb = self.wTb
        for tg in range(4):
            bk2 = self.pbank[1]
            ev = None
            for tt in range(4):
                t = tg * 4 + tt
                ev = S.emit("tensor", lambda tn, t=t, tt=tt, bk2=bk2:
                            tn.transpose(bk2[0:8, tt * 128:(tt + 1) * 128],
                                         self.wtok[:, t * 8:(t + 1) * 8],
                                         self.idf[:, :]),
                            waits=(self.bank_free.get(1, []) + [e13])
                            if tt == 0 else [],
                            tick=(tt == 3))
            e = S.emit("vector", lambda v, tg=tg, bk2=bk2, wTb=wTb:
                       v.tensor_copy(wTb[0:8, tg * 512:(tg + 1) * 512],
                                     bk2[0:8, 0:512]),
                       waits=[ev], tick=True)
            self.bank_free[1] = [e]
        S.fence()

    def ffn_phase(self, half):
        S, PB = self.S, self.PB
        ei = 0
        for tl in range(8):
            t = half * 8 + tl
            for och in range(2):
                bi = (2 * tl + och) % 6
                bk = self.pbank[bi]
                ev = S.emit("tensor", lambda tn, t=t, och=och, bk=bk:
                            tn.matmul(bk[:, 0:512],
                                      self.wTb[0:8, t * 128:(t + 1) * 128],
                                      self.besb[0:8, och * 512:(och + 1) * 512],
                                      start=True, stop=True),
                            waits=self.bank_free.get(bi, []) + [self.consts_g],
                            tick=True)
                ei += 1
                au = self.ACC[tl // 2]
                asl = slice((tl % 2) * 1024 + och * 512,
                            (tl % 2) * 1024 + (och + 1) * 512)
                e2 = S.emit("vector", lambda v, au=au, asl=asl, bk=bk:
                            v.tensor_tensor(au[:, asl], bk[:, 0:512], au[:, asl],
                                            ALU.add),
                            waits=[ev], tick=True)
                self.bank_free[bi] = [e2]
        S.fence()
        eu = {}
        ldev = {}
        for e2i in range(2):
            eu[e2i], ldev[e2i] = self._load_we(e2i)
        for ep in range(4):
            e0, e1 = 2 * ep, 2 * ep + 1
            u0, u1 = eu[e0], eu[e1]
            lev = [ldev[e0], ldev[e1]]
            if ep < 3:
                nxt0, nxt1 = 2 * ep + 2, 2 * ep + 3
                eu[nxt0], ldev[nxt0] = self._load_we(nxt0)
                eu[nxt1], ldev[nxt1] = self._load_we(nxt1)
            for tl in range(8):
                t = half * 8 + tl
                b0 = (tl * 4) % 6
                bidx = [(b0 + i) % 6 for i in range(4)]
                bks = [self.pbank[i] for i in bidx]
                ev = None
                w0 = sum((self.bank_free.get(i, []) for i in bidx), []) + lev
                for d in range(8):
                    for e2i, uu in ((0, u0), (1, u1)):
                        for och in range(2):
                            ev = S.emit("tensor", lambda tn, d=d, t=t, uu=uu,
                                        och=och, bk=bks[e2i * 2 + och]:
                                        tn.matmul(bk[:, 0:512],
                                                  self.xnT[:, d * NT + t * 128:
                                                           d * NT + (t + 1) * 128],
                                                  uu[d // 2][:, (d % 2) * 1024
                                                             + och * 512:
                                                             (d % 2) * 1024
                                                             + (och + 1) * 512],
                                                  start=(d == 0), stop=(d == 7)),
                                        waits=w0 if (d == 0 and e2i == 0 and och == 0)
                                        else [],
                                        tick=(d == 7 and e2i == 1 and och == 1))
                for e2i, ee in ((0, e0), (1, e1)):
                    for och in range(2):
                        au = self.ACC[tl // 2]
                        asl = slice((tl % 2) * 1024 + och * 512,
                                    (tl % 2) * 1024 + (och + 1) * 512)
                        e3 = S.emit("vector", lambda v, au=au, asl=asl, ee=ee, t=t,
                                    bk=bks[e2i * 2 + och]:
                                    v.scalar_tensor_tensor(
                                        au[:, asl], bk[:, 0:512],
                                        self.wtok[:, t * 8 + ee:t * 8 + ee + 1],
                                        au[:, asl], ALU.mult, ALU.add),
                                    waits=[ev], tick=True)
                        self.bank_free[bidx[e2i * 2 + och]] = [e3]
            for u in u0 + u1:
                PB.put(u)
        if half == 1:
            PB.put(self.wTb)

    def _load_we(self, e, waits=()):
        units = [self.PB.get(f"we{e}_{i}") for i in range(4)]
        ev = None
        for u in range(4):
            ev = self.S.dma("gpsimd", lambda g, un=units[u], u=u, e=e:
                            g.dma_start(
                                out=un.rearrange("p (a c) -> p a c", a=2),
                                in_=self.we_h[e, u * 256:(u + 1) * 256, :]
                                .rearrange("(a p) c -> p a c", p=128)),
                            waits=waits)
        return units, ev

    def out_half(self, half):
        S = self.S
        S.fence()
        for a in range(4):
            r0 = half * 1024 + a * 256
            S.dma("gpsimd", lambda g, a=a, r0=r0: g.dma_start(
                out=self.out_h[r0:r0 + 256, :]
                .rearrange("(i p) c -> p i c", p=128),
                in_=self.ACC[a].rearrange("p (i c) -> p i c", i=2),
                accum_op=ALU.add),
                waits=[self.xcopy_ev])
        S.fence(["gpsimd"])

    def finish(self):
        S = self.S
        S.fence()
        S.dma("sync", lambda sy: sy.dma_start(out=self.dbg_h[:, :],
                                              in_=self.dbgsb[:, :]),
              waits=[self.xcopy_ev])
        S.fence()
        final = []
        for e in ("sync", "gpsimd", "scalar"):
            if S.dcnt[e]:
                final.append((e + "_d", S.dcnt[e]))
        for e in ENGS:
            S.pending[e].extend(final)
        with self.nc.Block() as block:
            def runner(name):
                def body(eng):
                    for oi, (waits, fn, ticks, _d) in enumerate(S.ops[name]):
                        try:
                            for sk, val in waits:
                                eng.wait_ge(self.sems[sk], val)
                            inst = fn(eng)
                            for sk, amt in ticks:
                                assert inst is not None
                                inst = inst.then_inc(self.sems[sk], amt)
                        except Exception as ex:
                            raise RuntimeError(f"emit fail {name} op {oi}: {ex}") from ex
                    seen = {}
                    for sk, val in S.pending[name]:
                        if val > seen.get(sk, -1):
                            seen[sk] = val
                    for sk, val in seen.items():
                        if val > S.last_wait[name].get(sk, -1):
                            eng.wait_ge(self.sems[sk], val)
                return body
            block.tensor(runner("tensor"))
            block.vector(runner("vector"))
            block.scalar(runner("scalar"))
            block.gpsimd(runner("gpsimd"))
            block.sync(runner("sync"))
        self.ctx.close()
        return self.nc


def build_nc():
    return Builder().build()


def _consts():
    idbf = np.eye(128, dtype=BFNP)
    idf = np.eye(128, dtype=np.float32)
    hsel = np.zeros((128, 2), BFNP)
    hsel[0:64, 0] = 1
    hsel[64:128, 1] = 1
    p = np.arange(128)[:, None] % 64
    m = np.arange(64)[None, :]
    return dict(
        idbf=idbf, idf=idf, hsel=hsel,
        mstrict=(p > m).astype(BFNP),
        mustrict=(m > p).astype(BFNP),
        mutril=(m >= p).astype(BFNP),
        imaskr=(m == p).astype(BFNP),
    )


def make_in_maps(inputs):
    x = np.ascontiguousarray(np.asarray(inputs["L0_attn_x"], np.float32))
    consts = _consts()
    in_maps = []
    for c in range(NCORE):
        b, half = c // 2, c % 2
        recvm = np.zeros((128, 8), np.float32)
        if half == 1:
            recvm[:, c - 1] = 1.0
        m = dict(
            x=x[b, half * NT:(half + 1) * NT],
            wq=inputs["L0_attn_Wq"], wk=inputs["L0_attn_Wk"],
            wv=inputs["L0_attn_Wv"], wg=inputs["L0_attn_Wg"],
            wo=inputs["L0_attn_Wo"], wb=inputs["L0_attn_Wb"],
            wa=inputs["L0_attn_Wa"],
            dtb=np.asarray(inputs["L0_attn_dt_bias"], np.float32)[:, None],
            alog=np.asarray(inputs["L0_attn_A_log"], np.float32)[:, None],
            normw=np.tile(np.asarray(inputs["L0_attn_norm_w"], np.float32), 2)[:, None],
            we=inputs["L0_ffn_We"], be=inputs["L0_ffn_be"],
            wgf=inputs["L0_ffn_Wg"],
            bgf=np.asarray(inputs["L0_ffn_bg"], np.float32)[:, None],
            recvm=recvm, **consts,
        )
        in_maps.append({k: np.ascontiguousarray(np.asarray(v)) for k, v in m.items()})
    return in_maps


def kernel(**inputs):
    in_maps = make_in_maps(inputs)
    nc = build_nc()
    res = run_bass_kernel_spmd(nc, in_maps, core_ids=list(range(NCORE)))
    out = np.zeros((B, T, D), np.float32)
    for c in range(NCORE):
        b, half = c // 2, c % 2
        out[b, half * NT:(half + 1) * NT] = res.results[c]["out"]
    return out
